# version 02 (final)
# baseline (speedup 1.0000x reference)
import sys
sys.path.insert(0, '/opt/trn_rl_repo')
from contextlib import ExitStack

import numpy as np
import ml_dtypes

import concourse.bass as bass
import concourse.mybir as mybir
from concourse.bass_utils import run_bass_kernel_spmd

F32 = mybir.dt.float32
BF16 = mybir.dt.bfloat16
I32 = mybir.dt.int32
AF = mybir.ActivationFunctionType
ALU = mybir.AluOpType
AX = mybir.AxisListType

NCORES = 8
B, S, D = 4, 4096, 1024
TOK = 2048
NCH = D // 128


class _Rec:
    __slots__ = ('attr', 'a', 'k', 'incs')

    def __init__(self, attr, a, k):
        self.attr = attr
        self.a = a
        self.k = k
        self.incs = []

    def then_inc(self, sem, n=1):
        sem.n += n
        self.incs.append((sem, n))
        return self


class Sem:

    def __init__(self, handle, name):
        self.h = handle
        self.name = name
        self.n = 0


class _Eng:
    def __init__(self, prog, name):
        self._prog = prog
        self._name = name
        self.ops = []

    def __getattr__(self, attr):
        if attr.startswith('_'):
            raise AttributeError(attr)

        def call(*a, **k):
            rec = _Rec(attr, a, k)
            self.ops.append(rec)
            return rec
        return call

    def wait(self, sem, v=None):
        val = sem.n if v is None else v
        rec = _Rec('wait_ge', (sem, val), {})
        self.ops.append(rec)
        return rec


class Prog:
    def __init__(self, name):
        import os as _os0
        self.nc = bass.Bass("TRN2", target_bir_lowering=False,
                            detect_race_conditions=_os0.environ.get("RACEDET") == "1")
        self.es = ExitStack()
        self.name = name
        self.t = _Eng(self, 'tensor')
        self.v = _Eng(self, 'vector')
        self.s = _Eng(self, 'scalar')
        self.g = _Eng(self, 'gpsimd')
        self.y = _Eng(self, 'sync')
        self._sems = {}
        self._dram = {}

    def inp(self, name, shape, dtype=F32):
        ap = self.nc.dram_tensor(name, list(shape), dtype, kind="ExternalInput").ap()
        self._dram[name] = ap
        return ap

    def out(self, name, shape, dtype=F32):
        ap = self.nc.dram_tensor(name, list(shape), dtype, kind="ExternalOutput").ap()
        self._dram[name] = ap
        return ap

    def sb(self, name, shape, dtype=F32):
        return self.es.enter_context(self.nc.sbuf_tensor(name, list(shape), dtype))

    def ps(self, name, shape, dtype=F32):
        return self.es.enter_context(self.nc.psum_tensor(name, list(shape), dtype))

    def bank(self, n=1):
        if not hasattr(self, '_nbank'):
            self._nbank = 0
        out = []
        for _ in range(n):
            out.append(self.ps(f"bank{self._nbank}", [128, 512], F32))
            self._nbank += 1
        return out if n > 1 else out[0]

    def sem(self, name):
        if name not in self._sems:
            h = self.es.enter_context(self.nc.semaphore(name))
            self._sems[name] = Sem(h, name)
        return self._sems[name]

    def _fence_tiles(self):
        if not hasattr(self, '_ft'):
            self._ft = self.sb("_fence", [1, 8], F32)
            self.v.memset(self._ft[0:1, 0:2], 0.0)
            self.s.memzero(self._ft[0:1, 2:4])
        return self._ft

    def vfence(self, sem, n=1):
        ft = self._fence_tiles()
        return self.v.tensor_copy(ft[0:1, 0:1], ft[0:1, 1:2]).then_inc(sem, n)

    def sfence(self, sem, n=1):
        ft = self._fence_tiles()
        return self.s.activation(ft[0:1, 2:3], ft[0:1, 3:4], AF.Identity).then_inc(sem, n)

    def finish(self):
        def play(eng_rec):
            def body(eng):
                for rec in eng_rec.ops:
                    if rec.attr == 'wait_ge':
                        sem, val = rec.a
                        eng.wait_ge(sem.h, val)
                        continue
                    inst = getattr(eng, rec.attr)(*rec.a, **rec.k)
                    for sem, n in rec.incs:
                        inst.then_inc(sem.h, n)
            return body
        with self.nc.Block() as block:
            block.gpsimd(play(self.g))
            block.vector(play(self.v))
            block.tensor(play(self.t))
            block.scalar(play(self.s))
            block.sync(play(self.y))
        self.es.close()
        return self.nc


def _c(a):
    return np.ascontiguousarray(a)


def strips(wT, mtile=128):
    K, M = wT.shape
    assert M % mtile == 0
    return _c(wT.reshape(K, M // mtile, mtile).transpose(1, 0, 2))


def fm_slices(x_bsd):
    out = []
    for c in range(NCORES):
        b, h = c // 2, c % 2
        out.append(_c(x_bsd[b, h * TOK:(h + 1) * TOK, :].T))
    return out


def fm_unslice(parts):
    x = np.zeros((B, S, D), parts[0].dtype)
    for c in range(NCORES):
        b, h = c // 2, c % 2
        x[b, h * TOK:(h + 1) * TOK, :] = parts[c].T
    return x


def band_template(win_lo, win_hi, qblock=512):
    u = np.arange(-512, qblock + 512)
    kp = np.arange(128)
    dist = u[None, :] - kp[:, None]
    m = ((dist >= win_lo) & (dist <= win_hi)).astype(np.float32)
    return m.astype(ml_dtypes.bfloat16)


def run_spmd(prog, in_maps):
    if not getattr(prog, '_finished', False):
        prog.finish()
        prog._finished = True
    return run_bass_kernel_spmd(prog.nc, in_maps, core_ids=list(range(NCORES)))


BF = ml_dtypes.bfloat16

if __name__ == "__main__":
    print("kernel.py module (framework)")


class WPool:

    def __init__(self, P, name, width, nbuf=3):
        self.P = P
        self.width = width
        self.nbuf = nbuf
        self.tiles = [P.sb(f"wp_{name}_{i}", [128, width], BF16) for i in range(nbuf)]
        self.lds = [P.sem(f"wpl_{name}_{i}") for i in range(nbuf)]
        self.free = P.sem(f"wpf_{name}")
        self.count = 0


def emit_wload(P, pool, w_hbm, K, Mout):
    nk = K // 128
    nmt = Mout // 128
    assert nk * 128 <= pool.width
    out = []
    for mt in range(nmt):
        serial = pool.count
        pool.count += 1
        slot = serial % pool.nbuf
        if serial >= pool.nbuf:
            P.g.wait(pool.free, serial - pool.nbuf + 1)
        t = pool.tiles[slot]
        dst = bass.AP(t, 0, [[pool.width, 128], [128, nk], [1, 128]])
        srcv = w_hbm[mt].rearrange("(kc p) m -> p kc m", p=128)
        P.g.dma_start(out=dst, in_=srcv).then_inc(pool.lds[slot], 16)
        out.append((t, pool.lds[slot], 16 * (serial // pool.nbuf + 1), serial))
    return out, nk, nmt


def emit_gemm(P, wid, pool, w_hbm, K, Mout, xin_view, tokens, evac,
              tb_width=512, nsets=2, banks=None):
    loads, nk, nmt = emit_wload(P, pool, w_hbm, K, Mout)
    ntb = (tokens + tb_width - 1) // tb_width
    assert banks is not None and len(banks) >= nsets * ntb
    psums = banks
    mmdone = P.sem(f"mm_{wid}")
    ev_marks = {}
    for mt in range(nmt):
        t, ldsem, ldval, serial = loads[mt]
        P.t.wait(ldsem, ldval)
        if mt >= nsets and ev_marks.get(mt - nsets):
            for sem, val in ev_marks[mt - nsets]:
                P.t.wait(sem, val)
        pset = [psums[(mt % nsets) * ntb + i] for i in range(ntb)]
        rec = None
        for kc in range(nk):
            lhs = t[:, kc * 128:(kc + 1) * 128]
            for tb in range(ntb):
                t0 = tb * tb_width
                t1 = min(tokens, t0 + tb_width)
                rec = P.t.matmul(pset[tb][:, 0:t1 - t0], lhs, xin_view(kc, t0, t1),
                                 start=(kc == 0), stop=(kc == nk - 1))
        rec.then_inc(mmdone, 1)
        P.v.wait(mmdone, mmdone.n)
        P.v.nop().then_inc(pool.free, 1)
        marks = []
        for tb in range(ntb):
            t0 = tb * tb_width
            t1 = min(tokens, t0 + tb_width)
            marks.append(evac(mt, tb, pset[tb][:, 0:t1 - t0], mmdone, mmdone.n))
        ev_marks[mt] = marks
    return mmdone


def sb_view(t, offset, dims):
    return bass.AP(t, offset, dims)


class LNCtx:

    def __init__(self, P, tokens, nch=NCH):
        self.P = P
        self.tokens = tokens
        self.nch = nch
        self.xb = P.sb("ln_xb", [128, nch * tokens], BF16)
        self.sq = P.sb("ln_sq", [128, nch * tokens], BF16)
        self.mean = P.sb("ln_mean", [128, tokens], F32)
        self.rstd = P.sb("ln_rstd", [128, tokens], F32)
        self.row = P.sb("ln_row", [128, 1536], F32)
        self.banks = P._ln_banks if hasattr(P, '_ln_banks') else P.bank(4)
        self.psx, self.psq, self.pbm, self.pbr = self.banks
        self.ones128 = P.sb("ln_ones", [128, 1], BF16)
        self.onesK1 = P.sb("ln_onesk1", [1, 128], F32)
        self.eps = P.sb("ln_eps", [128, 1], F32)


def emit_ln(P, uid, ln, x_view, xn_out_view, eps=1e-5):
    tokens, nch = ln.tokens, ln.nch
    done = P.sem(f"lnd_{uid}")
    c0 = P.sem(f"lnc_{uid}")
    mmd = P.sem(f"lnm_{uid}")
    vd = P.sem(f"lnv_{uid}")
    src_is_xb = x_view is None
    if not src_is_xb:
        P.v.tensor_copy(ln.xb[:, :], x_view).then_inc(c0, 1)
        P.s.wait(c0, c0.n)
    P.s.activation(ln.sq[:, :], ln.xb[:, :], AF.Square).then_inc(c0, 1)
    ntb = tokens // 512
    for tb in range(ntb):
        P.t.wait(c0, c0.n)
        if tb >= 1:
            P.t.wait(vd, 4 * tb)
        rec = None
        for src, pdst in ((ln.xb, ln.psx), (ln.sq, ln.psq)):
            for kc in range(nch):
                rec = P.t.matmul(
                    pdst[0:1, :], ln.ones128[:, :],
                    bass.AP(src, kc * tokens + tb * 512, [[nch * tokens, 128], [1, 512]]),
                    start=(kc == 0), stop=(kc == nch - 1))
        rec.then_inc(mmd, 1)
        P.v.wait(mmd, mmd.n)
        mrow = bass.AP(ln.row, 0, [[1536, 1], [1, 512]])
        vrow = bass.AP(ln.row, 512, [[1536, 1], [1, 512]])
        rrow = bass.AP(ln.row, 1024, [[1536, 1], [1, 512]])
        psx = ln.psx[0:1, :]
        psq = ln.psq[0:1, :]
        P.v.tensor_scalar(mrow, psx, 1.0 / D, None, ALU.mult)
        P.v.tensor_scalar(vrow, psq, 1.0 / D, None, ALU.mult)
        P.v.tensor_tensor(psx, mrow, mrow, ALU.mult)
        P.v.tensor_tensor(vrow, vrow, psx, ALU.subtract).then_inc(vd, 1)
        P.s.wait(vd, vd.n)
        P.s.activation(rrow, vrow, AF.Sqrt, bias=ln.eps[0:1, 0:1]).then_inc(vd, 1)
        P.v.wait(vd, vd.n)
        P.v.reciprocal(rrow, rrow)
        P.vfence(vd, 1)
        P.t.wait(vd, vd.n)
        P.t.matmul(ln.pbm[:, :], ln.onesK1[0:1, :], mrow, start=True, stop=True)
        rec = P.t.matmul(ln.pbr[:, :], ln.onesK1[0:1, :], rrow, start=True, stop=True)
        rec.then_inc(mmd, 1)
        P.v.wait(mmd, mmd.n)
        P.v.tensor_copy(ln.mean[:, tb * 512:(tb + 1) * 512], ln.pbm[:, :])
        P.v.tensor_copy(ln.rstd[:, tb * 512:(tb + 1) * 512], ln.pbr[:, :]).then_inc(vd, 1)
    mean_b = bass.AP(ln.mean, 0, [[tokens, 128], [0, nch], [1, tokens]])
    rstd_b = bass.AP(ln.rstd, 0, [[tokens, 128], [0, nch], [1, tokens]])
    xb3 = bass.AP(ln.xb, 0, [[nch * tokens, 128], [tokens, nch], [1, tokens]])
    x3 = xb3 if src_is_xb else x_view.rearrange("p (c t) -> p c t", t=tokens)
    out3 = xn_out_view.rearrange("p (c t) -> p c t", t=tokens)
    P.v.wait(mmd, mmd.n)
    P.v.tensor_tensor(xb3, x3, mean_b, ALU.subtract)
    P.v.tensor_tensor(out3, xb3, rstd_b, ALU.mult)
    P.vfence(done, 1)
    return done


def chunk_view(t, nch, tokens, kc, t0, t1):
    return bass.AP(t, kc * tokens + t0, [[nch * tokens, 128], [1, t1 - t0]])


def chunk_viewer(t, nch, tokens):
    return lambda kc, t0, t1: chunk_view(t, nch, tokens, kc, t0, t1)


class ERing:

    def __init__(self, P, name, tokens, dtype):
        self.P = P
        self.tiles = [P.sb(f"er_{name}_{i}", [128, tokens], dtype) for i in range(2)]
        self.st = [P.sem(f"ers_{name}_{i}") for i in range(2)]
        self.uses = [0, 0]
        self.seq = 0

    def flush(self):
        self.P.y.wait(self.st[0], self.st[0].n)
        self.P.y.wait(self.st[1], self.st[1].n)


class EvacStore:

    def __init__(self, P, wid, out_hbm, tokens, func=None, bias=None, scale=1.0,
                 dtype=F32, engine='s', tb_width=512, col0=0, row0=0, ring=None):
        self.P, self.wid, self.out = P, wid, out_hbm
        self.tokens, self.func, self.bias, self.scale = tokens, func, bias, scale
        self.engine = engine
        self.col0, self.row0 = col0, row0
        self.ntb = (tokens + tb_width - 1) // tb_width
        self.ring = ring if ring is not None else ERing(P, wid, tokens, dtype)
        self.ev = P.sem(f"esv_{wid}")
        self.cur = None

    def __call__(self, mt, tb, psum_ap, mmsem, mmval):
        P = self.P
        eng = P.s if self.engine == 's' else P.v
        ring = self.ring
        if tb == 0:
            self.cur = ring.seq
            ring.seq += 1
        slot = self.cur % 2
        tile = ring.tiles[slot]
        t0 = tb * 512
        eng.wait(mmsem, mmval)
        if tb == 0 and ring.uses[slot] > 0:
            eng.wait(ring.st[slot], 16 * ring.uses[slot])
        w = psum_ap.shape[-1]
        dst = tile[:, t0:t0 + w]
        if self.engine == 's':
            kw = {}
            if self.bias is not None:
                kw['bias'] = self.bias(mt) if callable(self.bias) else self.bias
            kw['scale'] = self.scale(mt) if callable(self.scale) else self.scale
            rec = P.s.activation(dst, psum_ap, self.func or AF.Identity, **kw)
        else:
            rec = P.v.tensor_copy(dst, psum_ap)
        rec.then_inc(self.ev, 1)
        if tb == self.ntb - 1:
            if self.engine == 's':
                P.sfence(self.ev, 1)
            else:
                P.vfence(self.ev, 1)
            P.y.wait(self.ev, self.ev.n)
            P.y.dma_start(out=self.out[self.row0 + mt * 128:self.row0 + (mt + 1) * 128,
                                       self.col0:self.col0 + self.tokens],
                          in_=tile[:, 0:self.tokens]).then_inc(ring.st[slot], 16)
            ring.uses[slot] += 1
        return (self.ev, self.ev.n)

    def flush(self):
        self.ring.flush()


def load_fm(P, tile, hbm, rows, tokens, sem, cast=False, nch=None, row0=0):
    nch = nch or rows // 128
    eng = P.g if cast else P.y
    for kc in range(nch):
        eng.dma_start(out=chunk_view(tile, nch, tokens, kc, 0, tokens),
                      in_=hbm[row0 + kc * 128: row0 + (kc + 1) * 128, 0:tokens]).then_inc(sem, 16)
    return nch * 16


class EvacPersist:

    def __init__(self, P, wid, tile, tokens, out_hbm=None, func=None, bias=None,
                 scale=1.0, engine='s', base=0):
        self.P, self.wid, self.tile, self.tokens = P, wid, tile, tokens
        self.out = out_hbm
        self.func, self.bias, self.scale, self.engine = func, bias, scale, engine
        self.base = base
        self.ev = P.sem(f"epv_{wid}")
        self.st = P.sem(f"eps_{wid}")
        self.ntb = tokens // 512

    def __call__(self, mt, tb, psum_ap, mmsem, mmval):
        P = self.P
        eng = P.s if self.engine == 's' else P.v
        t0 = tb * 512
        w = psum_ap.shape[-1]
        eng.wait(mmsem, mmval)
        dst = bass.AP(self.tile, self.base + mt * self.tokens + t0,
                      [[self.tile.ap().ap[0][0], 128], [1, w]])
        if self.engine == 's':
            kw = {}
            if self.bias is not None:
                kw['bias'] = self.bias(mt) if callable(self.bias) else self.bias
            kw['scale'] = self.scale(mt) if callable(self.scale) else self.scale
            rec = P.s.activation(dst, psum_ap, self.func or AF.Copy, **kw)
        else:
            rec = P.v.tensor_copy(dst, psum_ap)
        rec.then_inc(self.ev, 1)
        if self.out is not None and tb == self.ntb - 1:
            if self.engine == 's':
                P.sfence(self.ev, 1)
            else:
                P.vfence(self.ev, 1)
            P.y.wait(self.ev, self.ev.n)
            P.y.dma_start(
                out=self.out[mt * 128:(mt + 1) * 128, 0:self.tokens],
                in_=bass.AP(self.tile, self.base + mt * self.tokens,
                            [[self.tile.ap().ap[0][0], 128], [1, self.tokens]])
            ).then_inc(self.st, 16)
        return (self.ev, self.ev.n)

    def flush(self):
        if self.out is not None:
            self.P.y.wait(self.st, self.st.n)


def emit_loadx_ln(P, tokens=TOK):
    x = P.inp("x", [D, tokens], F32)
    cst = P.inp("cst", [128, 4], F32)
    one_col = P.inp("one_col", [128, 1], F32)
    eps_col = P.inp("eps_col", [128, 1], F32)
    ones_row = P.inp("ones_row", [1, 128], F32)
    xnout = P.out("xn", [D, tokens], BF16)
    xn = P.sb("xn_t", [128, NCH * tokens], BF16)
    cstt = P.sb("cstt", [128, 4], F32)
    ln = LNCtx(P, tokens)
    ld = P.sem("ld_head")
    ldg = P.sem("ldg_head")
    P.y.dma_start(out=cstt[:, :], in_=cst[:, :]).then_inc(ld, 16)
    P.g.dma_start(out=ln.ones128[:, :], in_=one_col[:, :]).then_inc(ldg, 16)
    P.y.dma_start(out=ln.onesK1[:, :], in_=ones_row[:, :]).then_inc(ld, 16)
    P.y.dma_start(out=ln.eps[:, :], in_=eps_col[:, :]).then_inc(ld, 16)
    load_fm(P, ln.xb, x, D, tokens, ldg, cast=True)
    xt = None
    for eng in (P.v, P.s, P.t):
        eng.wait(ld, ld.n)
        eng.wait(ldg, ldg.n)
    done = emit_ln(P, "h", ln, None, xn[:, :])
    st = P.sem("st_head")
    P.y.wait(done, done.n)
    for kc in range(NCH):
        P.y.dma_start(out=xnout[kc * 128:(kc + 1) * 128, :],
                      in_=chunk_view(xn, NCH, tokens, kc, 0, tokens)).then_inc(st, 16)
    return xt, xn, ln, done, st, cstt


def host_head_inputs(x_fm):
    cst = np.zeros((128, 4), np.float32)
    cst[:, 0] = 1.0
    cst[:, 1] = 1e-5
    cst[:, 2] = 1e-12
    return {"x": x_fm, "cst": cst, "ones_row": np.ones((1, 128), np.float32),
            "one_col": np.ones((128, 1), np.float32),
            "eps_col": np.full((128, 1), 1e-5, np.float32)}


def build_lnproj3(nm="l1proj"):
    P = Prog(nm)
    bankpool = P.bank(8)
    P._ln_banks = bankpool[0:4]
    xt, xn, ln, done, st, cstt = emit_loadx_ln(P)
    outs = []
    for w in ("q", "k", "v"):
        wst = P.inp(f"w{w}_st", [8, 1024, 128], F32)
        o = P.out(w, [1024, TOK], BF16)
        outs.append((w, wst, o))
    P.t.wait(done, done.n)
    pool = WPool(P, "p", 8 * 128)
    ring = ERing(P, "sh", TOK, BF16)
    last_es = None
    for w, wst, o in outs:
        es = EvacStore(P, w, o, TOK, dtype=BF16, engine='v', ring=ring)
        if last_es is not None:
            P.t.wait(last_es.ev, last_es.ev.n)
        emit_gemm(P, w, pool, wst, 1024, 1024, chunk_viewer(xn, NCH, TOK), TOK, es,
                  nsets=2, banks=bankpool)
        last_es = es
    for _, _, _ in outs:
        pass
    last_es.flush()
    P.y.wait(P._sems["st_head"], P._sems["st_head"].n)
    return P


def build_l1attn():
    P = Prog("l1attn")
    SQ = 4096
    qf = P.inp("qf", [512, SQ], BF16)
    kf = P.inp("kf", [512, SQ], BF16)
    vt = P.inp("vt", [SQ, 520], BF16)
    ctpl = P.inp("ctpl", [128, 896], BF16)
    ones_row = P.inp("ones_row", [1, 128], F32)
    out = P.out("attn", [512, SQ], BF16)

    qt = P.sb("qt", [128, 4 * SQ], BF16)
    kt = P.sb("kt", [128, 4 * SQ], BF16)
    vtt = P.sb("vtt", [128, 32 * 520], BF16)
    tpl = P.sb("tpl", [128, 896], BF16)
    onesK1 = P.sb("onesK1", [1, 128], F32)
    pT = P.sb("pT", [128, 2 * 1024], BF16)
    attn_un = P.sb("attn_un", [65, SQ], F32)
    rrow = P.sb("rrow", [1, SQ], F32)
    rb = P.sb("rb", [64, SQ], F32)
    outt = P.sb("outt", [64, SQ], BF16)

    sc = [P.ps("sc0", [128, 1024], F32), P.ps("sc1", [128, 1024], F32)]
    pv = P.ps("pv", [128, 512], F32)
    pb = P.ps("pb", [128, 512], F32)

    ld = P.sem("ld")
    P.y.dma_start(out=tpl[:, :], in_=ctpl[:, :]).then_inc(ld, 16)
    P.y.dma_start(out=onesK1[:, :], in_=ones_row[:, :]).then_inc(ld, 16)
    load_fm(P, qt, qf, 512, SQ, ld)
    load_fm(P, kt, kf, 512, SQ, ld)
    P.y.dma_start(out=bass.AP(vtt, 0, [[32 * 520, 128], [520, 32], [1, 520]]),
                  in_=vt.rearrange("(t p) c -> p t c", p=128)).then_inc(ld, 16)
    P.t.wait(ld, ld.n)
    P.v.wait(ld, ld.n)
    P.s.wait(ld, ld.n)

    mmsc = P.sem("mmsc")
    exd = P.sem("exd")
    mskd = P.sem("mskd")
    pvd = P.sem("pvd")
    evd = P.sem("evd")
    nrm = P.sem("nrm")
    stt = P.sem("stt")

    NG = 2
    gseq = 0
    for h in range(8):
        ch, po = h // 2, (h % 2) * 64
        for qb in range(8):
            q0 = qb * 512
            nkt = 4 * qb + 4
            ngrp = (nkt + NG - 1) // NG
            qrhs = qt[po:po + 64, ch * SQ + q0: ch * SQ + q0 + 512]

            def emit_scores(g):
                n = gseq + g
                par = n % 2
                n_in_g = min(NG, nkt - g * NG)
                if n >= 2:
                    P.t.wait(exd, n - 1)
                rec = None
                for j in range(n_in_g):
                    kt_i = g * NG + j
                    klhs = kt[po:po + 64, ch * SQ + kt_i * 128: ch * SQ + kt_i * 128 + 128]
                    rec = P.t.matmul(sc[par][:, j * 512:(j + 1) * 512], klhs, qrhs,
                                     start=True, stop=True)
                rec.then_inc(mmsc, 1)
                P.s.wait(mmsc, n + 1)
                if n >= 2:
                    P.s.wait(pvd, n - 1)
                P.s.activation(pT[:, par * 1024: par * 1024 + n_in_g * 512],
                               sc[par][:, 0:n_in_g * 512], AF.Exp,
                               scale=1.0 / 64).then_inc(exd, 1)
                lastmask = None
                for j in range(n_in_g):
                    kt_i = g * NG + j
                    if 4 * qb <= kt_i:
                        d0 = kt_i * 128 - q0
                        if lastmask is None:
                            P.v.wait(exd, n + 1)
                        seg = pT[0:128, par * 1024 + j * 512: par * 1024 + (j + 1) * 512]
                        lastmask = P.v.tensor_tensor(
                            seg, seg, tpl[:, 384 - d0:384 - d0 + 512], ALU.mult)
                if lastmask is not None:
                    lastmask.then_inc(mskd, 1)
                    return ('msk', mskd.n)
                return ('exd', n + 1)

            def emit_pv(g, gate):
                n = gseq + g
                par = n % 2
                n_in_g = min(NG, nkt - g * NG)
                if gate[0] == 'msk':
                    P.t.wait(mskd, gate[1])
                else:
                    P.t.wait(exd, gate[1])
                rec = None
                for j in range(n_in_g):
                    kt_i = g * NG + j
                    vlhs = vtt[0:128, kt_i * 520 + h * 65: kt_i * 520 + (h + 1) * 65]
                    rec = P.t.matmul(pv[0:65, :], vlhs,
                                     pT[0:128, par * 1024 + j * 512: par * 1024 + (j + 1) * 512],
                                     start=(kt_i == 0), stop=(kt_i == nkt - 1))
                rec.then_inc(pvd, 1)

            gates = {}
            P.t.wait(evd, evd.n)
            for g in range(ngrp):
                gates[g] = emit_scores(g)
                if g >= 1:
                    emit_pv(g - 1, gates[g - 1])
            emit_pv(ngrp - 1, gates[ngrp - 1])
            gseq += ngrp
            P.v.wait(pvd, pvd.n)
            P.v.tensor_copy(attn_un[0:65, q0:q0 + 512], pv[0:65, :]).then_inc(evd, 1)
        P.v.wait(stt, stt.n)
        P.v.reciprocal(rrow[0:1, :], attn_un[64:65, :])
        P.vfence(nrm, 1)
        for tb in range(8):
            P.t.wait(nrm, nrm.n)
            P.t.matmul(pb[0:64, :], onesK1[0:1, 0:64], rrow[0:1, tb * 512:(tb + 1) * 512],
                       start=True, stop=True).then_inc(nrm, 1)
            P.v.wait(nrm, nrm.n)
            P.v.tensor_copy(rb[:, tb * 512:(tb + 1) * 512], pb[0:64, :]).then_inc(nrm, 1)
        P.v.tensor_tensor(outt[:, :], attn_un[0:64, :], rb[:, :], ALU.mult)
        P.vfence(nrm, 1)
        P.y.wait(nrm, nrm.n)
        P.y.dma_start(out=out[h * 64:(h + 1) * 64, :], in_=outt[0:64, :]).then_inc(stt, 16)
    P.y.wait(stt, stt.n)
    return P


def emit_wload2(P, poolA, poolB, wA_hbm, wB_hbm, K, Mout):
    nk = K // 128
    nmt = Mout // 128
    loadsA, loadsB = [], []
    for mt in range(nmt):
        for pool, w_hbm, acc in ((poolA, wA_hbm, loadsA), (poolB, wB_hbm, loadsB)):
            serial = pool.count
            pool.count += 1
            slot = serial % pool.nbuf
            if serial >= pool.nbuf:
                P.g.wait(pool.free, serial - pool.nbuf + 1)
            t = pool.tiles[slot]
            dst = bass.AP(t, 0, [[pool.width, 128], [128, nk], [1, 128]])
            srcv = w_hbm[mt].rearrange("(kc p) m -> p kc m", p=128)
            P.g.dma_start(out=dst, in_=srcv).then_inc(pool.lds[slot], 16)
            acc.append((t, pool.lds[slot], 16 * (serial // pool.nbuf + 1), serial))
    return loadsA, loadsB, nk, nmt


def emit_gemm2(P, wid, poolA, poolB, wA_hbm, wB_hbm, K, Mout, xin_view, tokens,
               combine, banks=None, nsets=2):
    loadsA, loadsB, nk, nmt = emit_wload2(P, poolA, poolB, wA_hbm, wB_hbm, K, Mout)
    ntb = tokens // 512
    assert banks is not None and len(banks) >= 2 * nsets * ntb
    mmdone = P.sem(f"mm_{wid}")
    ev_marks = {}
    for mt in range(nmt):
        tA, ldA, lvA, _ = loadsA[mt]
        tB, ldB, lvB, _ = loadsB[mt]
        P.t.wait(ldA, lvA)
        P.t.wait(ldB, lvB)
        if mt >= nsets and ev_marks.get(mt - nsets):
            for sem, val in ev_marks[mt - nsets]:
                P.t.wait(sem, val)
        s0 = (mt % nsets) * 2 * ntb
        psA = banks[s0:s0 + ntb]
        psB = banks[s0 + ntb:s0 + 2 * ntb]
        rec = None
        for w_t, pset in ((tA, psA), (tB, psB)):
            for kc in range(nk):
                lhs = w_t[:, kc * 128:(kc + 1) * 128]
                for tb in range(ntb):
                    rec = P.t.matmul(pset[tb][:, :], lhs,
                                     xin_view(kc, tb * 512, (tb + 1) * 512),
                                     start=(kc == 0), stop=(kc == nk - 1))
        rec.then_inc(mmdone, 1)
        P.v.wait(mmdone, mmdone.n)
        P.v.nop().then_inc(poolA.free, 1)
        P.v.nop().then_inc(poolB.free, 1)
        marks = []
        for tb in range(ntb):
            marks.append(combine(mt, tb, psA[tb], psB[tb], mmdone, mmdone.n))
        ev_marks[mt] = marks
    return mmdone


def build_l1tail():
    P = Prog("l1tail")
    bk = P.bank(8)
    xn = P.inp("xn", [1024, TOK], BF16)
    attn = P.inp("attn", [1024, TOK], BF16)
    s1o = P.out("s1", [1024, TOK], BF16)
    g1o = P.out("g1", [1024, TOK], BF16)
    wu = P.inp("wu_st", [8, 1024, 128], F32)
    w1 = P.inp("w1_st", [32, 1024, 128], F32)
    w2 = P.inp("w2_st", [32, 1024, 128], F32)
    h1 = P.inp("h1_st", [32, 4096, 128], F32)
    h2 = P.inp("h2_st", [32, 4096, 128], F32)
    w3 = P.inp("w3_st", [8, 4096, 128], F32)
    scl = P.inp("scl", [128, 8 + 32 * 4 + 8], F32)
    sclt = P.sb("sclt", [128, 8 + 32 * 4 + 8], F32)
    xnt = P.sb("xnt", [128, 8 * TOK], BF16)
    AR_W = 64 * 1024
    arena = P.sb("arena", [128, AR_W], BF16)
    HB, H1B, ATT = 0, 32 * 1024, 32 * 1024
    TH = 1024
    ta = P.sb("ta", [128, 512], BF16)
    tu = P.sb("tu", [128, 512], BF16)

    def av(base, tokens):
        return lambda kc, t0, t1: bass.AP(arena, base + kc * tokens + t0,
                                          [[AR_W, 128], [1, t1 - t0]])

    ld = P.sem("ld")
    P.y.dma_start(out=sclt[:, :], in_=scl[:, :]).then_inc(ld, 16)
    load_fm(P, xnt, xn, 1024, TOK, ld)
    for kc in range(NCH):
        P.y.dma_start(out=bass.AP(arena, ATT + kc * TOK, [[AR_W, 128], [1, TOK]]),
                      in_=attn[kc * 128:(kc + 1) * 128, :]).then_inc(ld, 16)
    P.t.wait(ld, ld.n)
    P.v.wait(ld, ld.n)
    P.s.wait(ld, ld.n)
    BU, S1, S2, H1S, H2S, S3 = 0, 8, 40, 72, 104, 136

    pool = WPool(P, "m", 32 * 128, nbuf=2)
    poolA = WPool(P, "a", 8 * 128, nbuf=2)
    poolB = WPool(P, "b", 8 * 128, nbuf=2)

    es1 = EvacStore(P, "wu", s1o, TOK, func=AF.Identity,
                    bias=lambda mt: sclt[:, BU + mt:BU + mt + 1], scale=1.0,
                    dtype=BF16, engine='s')
    emit_gemm(P, "wu", pool, wu, 1024, 1024, av(ATT, TOK), TOK, es1,
              nsets=2, banks=bk)
    es1.flush()

    cmb = P.sem("cmb")
    for half in range(2):
        off = half * TH
        hof = "ab"[half]
        xv = lambda kc, t0, t1, off=off: chunk_view(xnt, 8, TOK, kc, off + t0, off + t1)
        P.t.wait(es1.ev, es1.ev.n)
        if half == 1:
            P.t.wait(cmb, cmb.n)

        def combine(mt, tb, psA, psB, mmsem, mmval):
            P.s.wait(cmb, cmb.n)
            P.s.wait(mmsem, mmval)
            P.s.activation(ta[:, :], psA[:, :], AF.Silu,
                           scale=sclt[:, S1 + mt:S1 + mt + 1])
            P.s.activation(tu[:, :], psB[:, :], AF.Identity,
                           scale=sclt[:, S2 + mt:S2 + mt + 1]).then_inc(cmb, 1)
            P.v.wait(cmb, cmb.n)
            P.v.tensor_tensor(
                bass.AP(arena, HB + mt * TH + tb * 512, [[AR_W, 128], [1, 512]]),
                ta[:, :], tu[:, :], ALU.mult).then_inc(cmb, 1)
            return (cmb, cmb.n)
        emit_gemm2(P, f"g2{hof}", poolA, poolB, w1, w2, 1024, 4096, xv, TH, combine,
                   banks=bk)
        P.t.wait(cmb, cmb.n)
        eh1 = EvacPersist(P, f"h1{hof}", arena, TH, base=H1B, func=AF.Gelu,
                          scale=lambda mt: sclt[:, H1S + mt:H1S + mt + 1], engine='s')
        emit_gemm(P, f"h1{hof}", pool, h1, 4096, 4096, av(HB, TH), TH, eh1,
                  nsets=2, banks=bk)
        P.t.wait(eh1.ev, eh1.ev.n)
        eh2 = EvacPersist(P, f"h2{hof}", arena, TH, base=HB, func=AF.Gelu,
                          scale=lambda mt: sclt[:, H2S + mt:H2S + mt + 1], engine='s')
        emit_gemm(P, f"h2{hof}", pool, h2, 4096, 4096, av(H1B, TH), TH, eh2,
                  nsets=2, banks=bk)
        P.t.wait(eh2.ev, eh2.ev.n)
        eg = EvacStore(P, f"w3{hof}", g1o, TH, func=AF.Silu,
                       scale=lambda mt: sclt[:, S3 + mt:S3 + mt + 1], dtype=BF16,
                       engine='s', col0=off)
        emit_gemm(P, f"w3{hof}", pool, w3, 4096, 1024, av(HB, TH), TH, eg,
                  nsets=2, banks=bk)
        eg.flush()
        P.t.wait(eg.ev, eg.ev.n)
    return P


def build_blockfin():
    P = Prog("blockfin")
    x = P.inp("x", [1024, TOK], F32)
    s = P.inp("s", [1024, TOK], BF16)
    g = P.inp("g", [1024, TOK], F32)
    xo = P.out("xo", [1024, TOK], F32)
    xs = [P.sb(f"fx{i}", [128, TOK], F32) for i in range(2)]
    ss = [P.sb(f"fs{i}", [128, TOK], BF16) for i in range(2)]
    gs = [P.sb(f"fg{i}", [128, TOK], F32) for i in range(2)]
    oo = [P.sb(f"fo{i}", [128, TOK], F32) for i in range(2)]
    lds = [P.sem(f"ld{i}") for i in range(2)]
    sts = [P.sem(f"st{i}") for i in range(2)]
    cd = P.sem("cd")
    for kc in range(NCH):
        par = kc % 2
        if kc >= 2:
            P.y.wait(cd, cd.n - 1)
        P.y.dma_start(out=xs[par][:, :], in_=x[kc * 128:(kc + 1) * 128, :]).then_inc(lds[par], 16)
        P.y.dma_start(out=ss[par][:, :], in_=s[kc * 128:(kc + 1) * 128, :]).then_inc(lds[par], 16)
        P.y.dma_start(out=gs[par][:, :], in_=g[kc * 128:(kc + 1) * 128, :]).then_inc(lds[par], 16)
        P.v.wait(lds[par], lds[par].n)
        if kc >= 2:
            P.v.wait(sts[par], sts[par].n)
        P.v.tensor_tensor(oo[par][:, :], ss[par][:, :], gs[par][:, :], ALU.mult)
        P.v.tensor_tensor(oo[par][:, :], oo[par][:, :], xs[par][:, :], ALU.add)
        P.vfence(cd, 1)
        P.y.wait(cd, cd.n)
        P.y.dma_start(out=xo[kc * 128:(kc + 1) * 128, :], in_=oo[par][:, :]).then_inc(sts[par], 16)
    P.y.wait(sts[0], sts[0].n)
    P.y.wait(sts[1], sts[1].n)
    return P


def causal_tpl():
    c = np.arange(896)[None, :] - 384
    kp = np.arange(128)[:, None]
    return (c >= kp).astype(np.float32).astype(BF)


def vec_fm(v, nmt=None):
    v = np.asarray(v, np.float32).ravel()
    nmt = nmt or v.size // 128
    return _c(v.reshape(nmt, 128).T)


def l1_host(inputs, x1_fm):
    wq_st = strips(inputs['L1_attn_w_q'])
    wk_st = strips(inputs['L1_attn_w_k'])
    wv_st = strips(inputs['L1_attn_w_v'])
    proj_maps = []
    for c in range(NCORES):
        m = host_head_inputs(x1_fm[c])
        m.update({"wq_st": wq_st, "wk_st": wk_st, "wv_st": wv_st})
        proj_maps.append(m)
    return proj_maps


def l1_attn_maps(qs, ks, vs):
    tpl = causal_tpl()
    ones_row = np.ones((1, 128), np.float32)
    maps = []
    for c in range(NCORES):
        b, hg = c // 2, c % 2
        qfull = np.concatenate([qs[2 * b], qs[2 * b + 1]], axis=1)
        kfull = np.concatenate([ks[2 * b], ks[2 * b + 1]], axis=1)
        vfull = np.concatenate([vs[2 * b], vs[2 * b + 1]], axis=1)
        qf = _c(qfull[hg * 512:(hg + 1) * 512])
        kf = _c(kfull[hg * 512:(hg + 1) * 512])
        vT = vfull.T
        vt = np.zeros((S, 8, 65), dtype=BF)
        for j in range(8):
            gh = hg * 8 + j
            vt[:, j, 0:64] = vT[:, gh * 64:(gh + 1) * 64]
        vt[:, :, 64] = np.float32(1.0)
        maps.append({"qf": qf, "kf": kf, "vt": _c(vt.reshape(S, 520)),
                     "ctpl": tpl, "ones_row": ones_row})
    return maps


def l1_tail_maps(inputs, xns, attns):
    wu_st = strips(inputs['L1_attn_w_u'])
    w1_st = strips(_c(inputs['L1_ffn_w1_w'].T))
    w2_st = strips(_c(inputs['L1_ffn_w2_w'].T))
    h1_st = strips(_c(inputs['L1_ffn_h1_w'].T))
    h2_st = strips(_c(inputs['L1_ffn_h2_w'].T))
    w3_st = strips(_c(inputs['L1_ffn_w3_w'].T))
    scl = np.concatenate([
        vec_fm(inputs['L1_attn_b_u']), vec_fm(inputs['L1_ffn_w1_s']),
        vec_fm(inputs['L1_ffn_w2_s']), vec_fm(inputs['L1_ffn_h1_s']),
        vec_fm(inputs['L1_ffn_h2_s']), vec_fm(inputs['L1_ffn_w3_s'])], axis=1)
    maps = []
    for c in range(NCORES):
        b, half = c // 2, c % 2
        afull = np.concatenate([attns[2 * b], attns[2 * b + 1]], axis=0)
        attn = _c(afull[:, half * TOK:(half + 1) * TOK])
        maps.append({"xn": xns[c], "attn": attn, "wu_st": wu_st, "w1_st": w1_st,
                     "w2_st": w2_st, "h1_st": h1_st, "h2_st": h2_st, "w3_st": w3_st,
                     "scl": scl})
    return maps


def build_l0proj():
    P = Prog("l0proj")
    bankpool = P.bank(8)
    P._ln_banks = bankpool[0:4]
    xt, xn, ln, done, st, cstt = emit_loadx_ln(P)
    mr = P.out("mr", [2, TOK], F32)
    P.y.wait(done, done.n)
    strow = P.sem("strow")
    P.y.dma_start(out=mr[0:1, :], in_=ln.mean[0:1, :]).then_inc(strow, 16)
    P.y.dma_start(out=mr[1:2, :], in_=ln.rstd[0:1, :]).then_inc(strow, 16)

    pool = WPool(P, "p", 8 * 128)
    ring = ERing(P, "sh", TOK, BF16)
    xv = chunk_viewer(xn, NCH, TOK)
    P.t.wait(done, done.n)

    qp = ln.sq
    wq = P.inp("wq_st", [8, 1024, 128], F32)
    qo = P.out("q", [1024, TOK], BF16)
    eq = EvacPersist(P, "q", qp, TOK, out_hbm=qo, engine='v')
    emit_gemm(P, "q", pool, wq, 1024, 1024, xv, TOK, eq, nsets=2, banks=bankpool)

    prev = eq
    outs = []
    for w, mout in (("k", 1024), ("v", 1024), ("gq", 512), ("gkp", 512), ("gvp", 512)):
        wst = P.inp(f"w{w}_st", [mout // 128, 1024, 128], F32)
        outs.append((w, mout, wst))
    gkp = P.sb("gkp", [128, 4 * TOK], BF16)
    gvp = P.sb("gvp", [128, 4 * TOK], BF16)
    for w, mout, wst in outs:
        P.t.wait(prev.ev, prev.ev.n)
        if w in ("gkp", "gvp"):
            tile = gkp if w == "gkp" else gvp
            es = EvacPersist(P, w, tile, TOK, engine='v')
        else:
            o = P.out(w, [mout, TOK], BF16)
            es = EvacStore(P, w, o, TOK, dtype=BF16, engine='v', ring=ring)
        emit_gemm(P, w, pool, wst, 1024, mout, xv, TOK, es, nsets=2, banks=bankpool)
        prev = es

    poold = P.sem("poold")
    glo = P.out("gkl", [512, 32], F32)
    gvo = P.out("gvl", [512, 32], F32)
    pooled = P.sb("pooled", [128, 4 * 32 * 2], F32)
    P.v.wait(prev.ev, prev.ev.n)
    for idx, (tile, o) in enumerate(((gkp, glo), (gvp, gvo))):
        src = bass.AP(tile, 0, [[4 * TOK, 128], [TOK, 4], [64, 32], [1, 64]])
        dst = bass.AP(pooled, idx * 128, [[256, 128], [32, 4], [1, 32], [1, 1]])
        P.v.reduce_sum(dst, src, AX.X)
        P.v.tensor_scalar(bass.AP(pooled, idx * 128, [[256, 128], [1, 128]]),
                          bass.AP(pooled, idx * 128, [[256, 128], [1, 128]]),
                          1.0 / 64, None, ALU.mult)
        P.vfence(poold, 1)
        P.y.wait(poold, poold.n)
        P.y.dma_start(out=o.rearrange("(c p) l -> p c l", p=128),
                      in_=bass.AP(pooled, idx * 128, [[256, 128], [32, 4], [1, 32]])
                      ).then_inc(strow, 16)

    g2 = P.inp("g2_st", [1, 1024, 128], F32)
    glout = P.out("gl", [16, TOK], F32)
    grout = P.out("gr", [16, TOK], F32)
    gsig = P.sb("gsig", [64, TOK], F32)
    gdone = P.sem("gdone")

    def gevac(mt, tb, psum_ap, mmsem, mmval):
        P.s.wait(mmsem, mmval)
        P.s.activation(gsig[0:64, tb * 512:(tb + 1) * 512], psum_ap[0:64, :],
                       AF.Sigmoid).then_inc(gdone, 1)
        if tb == 3:
            P.sfence(gdone, 1)
        return (gdone, gdone.n)
    P.t.wait(prev.ev, prev.ev.n)
    P.t.wait(eq.ev, eq.ev.n)
    emit_gemm(P, "g2", pool, g2, 1024, 128, chunk_viewer(qp, NCH, TOK), TOK, gevac,
              nsets=1, banks=bankpool)
    P.y.wait(gdone, gdone.n)
    P.y.dma_start(out=glout[:, :], in_=gsig[0:16, :]).then_inc(strow, 16)
    P.y.dma_start(out=grout[:, :], in_=gsig[32:48, :]).then_inc(strow, 16)

    ring.flush()
    P.y.wait(strow, strow.n)
    P.y.wait(P._sems["st_head"], P._sems["st_head"].n)
    return P


QMID_HI = 1.25

def emit_quant(P, uid, src_ap, dst_ap, scr, width, sgn=None):
    nb = width // 16
    qd = P.sem(f"q_{uid}")
    t, it = scr['t'], scr['i']
    bm, rs = scr['bm'], scr['rs']
    v, s = P.v, P.s
    v.reduce_max(bass.AP(bm, 0, [[scr['bmw'], 128], [1, nb], [1, 1]]),
                 src_ap.rearrange("p (b j) -> p b j", j=16), AX.X,
                 apply_absolute_value=True)
    v.tensor_scalar(bm[:, 0:nb], bm[:, 0:nb], 1.0 / 6, 1e-12, ALU.mult, ALU.max)
    v.reciprocal(rs[:, 0:nb], bm[:, 0:nb])
    rs_exp = bass.AP(rs, 0, [[scr['bmw'], 128], [1, nb], [0, 16]])
    v.tensor_tensor(t[:, 0:width].rearrange("p (b j) -> p b j", j=16),
                    src_ap.rearrange("p (b j) -> p b j", j=16), rs_exp, ALU.mult).then_inc(qd, 1)
    s.wait(qd, qd.n)
    s.activation(it[:, 0:width], t[:, 0:width], AF.Abs).then_inc(qd, 1)
    v.wait(qd, qd.n)
    ii = it.ap().bitcast(I32)[:, 0:width]
    v.tensor_single_scalar(ii, ii, 0x00200000, ALU.add)
    v.tensor_single_scalar(ii, ii, np.int32(np.uint32(0xFFC00000).view(np.int32)).item(),
                           ALU.bitwise_and)
    A = it[:, 0:width]
    v.tensor_scalar(A, A, 6.0, None, ALU.min)
    sq = scr['t2']
    v.tensor_tensor(sq[:, 0:width], t[:, 0:width], t[:, 0:width], ALU.mult)
    low = scr['t3']
    v.tensor_single_scalar(low[:, 0:width], sq[:, 0:width], 0.0625, ALU.is_gt)
    v.scalar_tensor_tensor(low[:, 0:width], sq[:, 0:width], 0.5625, low[:, 0:width],
                           ALU.is_gt, ALU.add)
    v.tensor_scalar(low[:, 0:width], low[:, 0:width], 0.5, None, ALU.mult)
    ind = sq
    v.tensor_single_scalar(ind[:, 0:width], sq[:, 0:width], QMID_HI * QMID_HI, ALU.is_gt)
    v.tensor_tensor(A, A, ind[:, 0:width], ALU.mult)
    v.tensor_tensor(A, A, low[:, 0:width], ALU.max).then_inc(qd, 1)
    s.wait(qd, qd.n)
    s.activation(sq[:, 0:width], t[:, 0:width], AF.Sign).then_inc(qd, 1)
    v.wait(qd, qd.n)
    v.tensor_tensor(A, A, sq[:, 0:width], ALU.mult)
    bm_exp = bass.AP(bm, 0, [[scr['bmw'], 128], [1, nb], [0, 16]])
    v.tensor_tensor(dst_ap.rearrange("p (b j) -> p b j", j=16),
                    A.rearrange("p (b j) -> p b j", j=16), bm_exp, ALU.mult)
    P.vfence(qd, 1)
    return qd


def build_l0wq():
    P = Prog("l0wq")
    wi = P.inp("wrows", [1152, 1024], F32)
    wo = P.out("wq", [1152, 1024], BF16)
    W = 1024
    xt = P.sb("xt", [128, W], F32)
    ot = P.sb("ot", [128, W], BF16)
    scr = {'t': P.sb("t", [128, W], F32), 'i': P.sb("i", [128, W], F32),
           't2': P.sb("t2", [128, W], F32), 't3': P.sb("t3", [128, W], F32),
           'bm': P.sb("bm", [128, W // 16], F32), 'rs': P.sb("rs", [128, W // 16], F32),
           'bmw': W // 16}
    ld = P.sem("ld")
    stq = P.sem("stq")
    for r in range(9):
        P.y.wait(stq, stq.n)
        P.y.dma_start(out=xt[:, :], in_=wi[r * 128:(r + 1) * 128, :]).then_inc(ld, 16)
        P.v.wait(ld, ld.n)
        qd = emit_quant(P, f"r{r}", xt[:, :], ot[:, :], scr, W)
        P.y.wait(qd, qd.n)
        P.y.dma_start(out=wo[r * 128:(r + 1) * 128, :], in_=ot[:, :]).then_inc(stq, 16)
    P.y.wait(stq, stq.n)
    return P


def emit_transpose_block(P, tp_ps, ident, src_ap, dst_ap, sem, evac_eng='v'):
    rec = P.t.transpose(tp_ps, src_ap, ident)
    rec.then_inc(sem, 1)
    eng = P.v if evac_eng == 'v' else P.s
    eng.wait(sem, sem.n)
    if evac_eng == 'v':
        eng.tensor_copy(dst_ap, tp_ps)
        P.vfence(sem, 1)
    else:
        eng.activation(dst_ap, tp_ps, AF.Identity)
        P.sfence(sem, 1)
    P.t.wait(sem, sem.n)


def build_l0xq():
    P = Prog("l0xq")
    x = P.inp("x", [1024, TOK], F32)
    mr = P.inp("mr", [2, TOK], F32)
    idf = P.inp("idf", [128, 128], F32)
    ones_row = P.inp("ones_row", [1, 128], F32)
    xqo = P.out("xq", [1024, TOK], BF16)
    idft = P.sb("idft", [128, 128], F32)
    idbt = P.sb("idbt", [128, 128], BF16)
    onesK1 = P.sb("onesK1", [1, 128], F32)
    mrow = P.sb("mrow", [1, 2 * TOK], F32)
    mb = P.sb("mb", [128, 512], F32)
    rb = P.sb("rb", [128, 512], F32)
    xsl = P.sb("xsl", [128, 8 * 512], F32)
    xtm = P.sb("xtm", [128, 4 * 1024], F32)
    xqtm = P.sb("xqtm", [128, 4 * 1024], BF16)
    xqsl = P.sb("xqsl", [128, 8 * 512], BF16)
    W = 4 * 1024
    scr = {'t': P.sb("sct", [128, W], F32), 'i': P.sb("sci", [128, W], F32),
           't2': P.sb("sct2", [128, W], F32), 't3': P.sb("sct3", [128, W], F32),
           'bm': P.sb("scbm", [128, W // 16], F32), 'rs': P.sb("scrs", [128, W // 16], F32),
           'bmw': W // 16}
    ptf = P.ps("ptf", [128, 512], F32)
    ptb = P.ps("ptb", [128, 1024], BF16)
    pb = P.ps("pbq", [128, 512], F32)
    ld = P.sem("ld"); ldg = P.sem("ldg"); tsp = P.sem("tsp"); bcd = P.sem("bcd")
    stq = P.sem("stq")
    P.y.dma_start(out=idft[:, :], in_=idf[:, :]).then_inc(ld, 16)
    P.g.dma_start(out=idbt[:, :], in_=idf[:, :]).then_inc(ldg, 16)
    P.y.dma_start(out=onesK1[:, :], in_=ones_row[:, :]).then_inc(ld, 16)
    P.y.dma_start(out=bass.AP(mrow, 0, [[2 * TOK, 1], [1, TOK]]), in_=mr[0:1, :]).then_inc(ld, 16)
    P.y.dma_start(out=bass.AP(mrow, TOK, [[2 * TOK, 1], [1, TOK]]), in_=mr[1:2, :]).then_inc(ld, 16)
    for e in (P.t, P.v, P.s):
        e.wait(ld, ld.n)
        e.wait(ldg, ldg.n)
    xld = P.sem("xld")
    for sl in range(4):
        t0 = sl * 512
        P.y.wait(tsp, tsp.n)
        for kc in range(NCH):
            P.y.dma_start(out=bass.AP(xsl, kc * 512, [[8 * 512, 128], [1, 512]]),
                          in_=x[kc * 128:(kc + 1) * 128, t0:t0 + 512]).then_inc(xld, 16)
        P.t.wait(xld, xld.n)
        P.t.matmul(pb[:, :], onesK1[0:1, :], bass.AP(mrow, t0, [[2 * TOK, 1], [1, 512]]),
                   start=True, stop=True).then_inc(bcd, 1)
        P.v.wait(bcd, bcd.n)
        P.v.tensor_copy(mb[:, :], pb[:, :]).then_inc(bcd, 1)
        P.t.wait(bcd, bcd.n)
        P.t.matmul(pb[:, :], onesK1[0:1, :], bass.AP(mrow, TOK + t0, [[2 * TOK, 1], [1, 512]]),
                   start=True, stop=True).then_inc(bcd, 1)
        P.v.wait(bcd, bcd.n)
        P.v.tensor_copy(rb[:, :], pb[:, :]).then_inc(bcd, 1)
        x3 = bass.AP(xsl, 0, [[8 * 512, 128], [512, 8], [1, 512]])
        P.v.wait(xld, xld.n)
        P.v.tensor_tensor(x3, x3, bass.AP(mb, 0, [[512, 128], [0, 8], [1, 512]]), ALU.subtract)
        P.v.tensor_tensor(x3, x3, bass.AP(rb, 0, [[512, 128], [0, 8], [1, 512]]), ALU.mult).then_inc(bcd, 1)
        P.t.wait(bcd, bcd.n)
        for tt in range(4):
            for kc in range(NCH):
                emit_transpose_block(P, ptf[:, 0:128], idft[:, :],
                                     bass.AP(xsl, kc * 512 + tt * 128, [[8 * 512, 128], [1, 128]]),
                                     bass.AP(xtm, tt * 1024 + kc * 128, [[4 * 1024, 128], [1, 128]]),
                                     tsp)
        qd = emit_quant(P, f"xq{sl}", xtm[:, :], xqtm[:, :], scr, 4 * 1024)
        P.t.wait(qd, qd.n)
        P.v.wait(stq, stq.n)
        for tt in range(4):
            for kc in range(NCH):
                emit_transpose_block(P, ptb[:, 0:128], idbt[:, :],
                                     bass.AP(xqtm, tt * 1024 + kc * 128, [[4 * 1024, 128], [1, 128]]),
                                     bass.AP(xqsl, kc * 512 + tt * 128, [[8 * 512, 128], [1, 128]]),
                                     tsp)
        P.y.wait(tsp, tsp.n)
        for kc in range(NCH):
            P.y.dma_start(out=xqo[kc * 128:(kc + 1) * 128, t0:t0 + 512],
                          in_=bass.AP(xqsl, kc * 512, [[8 * 512, 128], [1, 512]])).then_inc(stq, 16)
    P.y.wait(stq, stq.n)
    return P


def build_l0gu():
    P = Prog("l0gu")
    bk = P.bank(8)
    xqi = P.inp("xq", [1024, TOK], BF16)
    wgq = P.inp("wgq_st", [22, 1024, 128], BF16)
    wuq = P.inp("wuq_st", [22, 1024, 128], BF16)
    ho = P.out("h", [2816, TOK], F32)
    xq = P.sb("xqt", [128, 8 * TOK], BF16)
    ta = P.sb("taq", [128, 512], F32)
    tu = P.sb("tuq", [128, 512], F32)
    hsl = [P.sb(f"hsl{i}", [128, TOK], F32) for i in range(2)]
    ld = P.sem("ld")
    load_fm(P, xq, xqi, 1024, TOK, ld)
    P.t.wait(ld, ld.n)
    P.v.wait(ld, ld.n)
    P.s.wait(ld, ld.n)
    poolA = WPool(P, "a", 8 * 128, nbuf=2)
    poolB = WPool(P, "b", 8 * 128, nbuf=2)
    cmb = P.sem("cmb")
    hst = [P.sem(f"hst{i}") for i in range(2)]
    uses = [0, 0]

    def combine(mt, tb, psA, psB, mmsem, mmval):
        P.s.wait(cmb, cmb.n)
        P.s.wait(mmsem, mmval)
        P.s.activation(ta[:, :], psA[:, :], AF.Silu)
        P.s.activation(tu[:, :], psB[:, :], AF.Identity).then_inc(cmb, 1)
        P.v.wait(cmb, cmb.n)
        if tb == 0 and uses[mt % 2] > 0:
            P.v.wait(hst[mt % 2], 16 * uses[mt % 2])
        P.v.tensor_tensor(hsl[mt % 2][:, tb * 512:(tb + 1) * 512], ta[:, :], tu[:, :],
                          ALU.mult).then_inc(cmb, 1)
        if tb == 3:
            P.y.wait(cmb, cmb.n)
            P.y.dma_start(out=ho[mt * 128:(mt + 1) * 128, :],
                          in_=hsl[mt % 2][:, :]).then_inc(hst[mt % 2], 16)
            uses[mt % 2] += 1
        return (cmb, cmb.n)
    emit_gemm2(P, "gu", poolA, poolB, wgq, wuq, 1024, 2816, chunk_viewer(xq, 8, TOK),
               TOK, combine, banks=bk, nsets=1)
    P.y.wait(hst[0], hst[0].n)
    P.y.wait(hst[1], hst[1].n)
    return P


def build_l0hq():
    P = Prog("l0hq")
    h = P.inp("h", [2816, TOK], F32)
    idf = P.inp("idf", [128, 128], F32)
    hqo = P.out("hq", [2816, TOK], BF16)
    idft = P.sb("idft", [128, 128], F32)
    idbt = P.sb("idbt", [128, 128], BF16)
    hsl = P.sb("hsl", [128, 22 * 256], F32)
    htm = P.sb("htm", [128, 2 * 2816], F32)
    hqtm = P.sb("hqtm", [128, 2 * 2816], BF16)
    hqsl = P.sb("hqsl", [128, 22 * 256], BF16)
    W = 2 * 2816
    scr = {'t': P.sb("sct", [128, W], F32), 'i': P.sb("sci", [128, W], F32),
           't2': P.sb("sct2", [128, W], F32), 't3': P.sb("sct3", [128, W], F32),
           'bm': P.sb("scbm", [128, W // 16], F32), 'rs': P.sb("scrs", [128, W // 16], F32),
           'bmw': W // 16}
    ptf = P.ps("ptf", [128, 512], F32)
    ptb = P.ps("ptb", [128, 1024], BF16)
    ld = P.sem("ld"); ldg = P.sem("ldg"); tsp = P.sem("tsp"); stq = P.sem("stq")
    hld = P.sem("hld")
    P.y.dma_start(out=idft[:, :], in_=idf[:, :]).then_inc(ld, 16)
    P.g.dma_start(out=idbt[:, :], in_=idf[:, :]).then_inc(ldg, 16)
    for e in (P.t, P.v, P.s):
        e.wait(ld, ld.n)
        e.wait(ldg, ldg.n)
    for sl in range(8):
        t0 = sl * 256
        P.y.wait(tsp, tsp.n)
        for kc in range(22):
            P.y.dma_start(out=bass.AP(hsl, kc * 256, [[22 * 256, 128], [1, 256]]),
                          in_=h[kc * 128:(kc + 1) * 128, t0:t0 + 256]).then_inc(hld, 16)
        P.t.wait(hld, hld.n)
        for tt in range(2):
            for kc in range(22):
                emit_transpose_block(P, ptf[:, 0:128], idft[:, :],
                                     bass.AP(hsl, kc * 256 + tt * 128, [[22 * 256, 128], [1, 128]]),
                                     bass.AP(htm, tt * 2816 + kc * 128, [[2 * 2816, 128], [1, 128]]),
                                     tsp)
        qd = emit_quant(P, f"h{sl}", htm[:, :], hqtm[:, :], scr, W)
        P.t.wait(qd, qd.n)
        P.v.wait(stq, stq.n)
        for tt in range(2):
            for kc in range(22):
                emit_transpose_block(P, ptb[:, 0:128], idbt[:, :],
                                     bass.AP(hqtm, tt * 2816 + kc * 128, [[2 * 2816, 128], [1, 128]]),
                                     bass.AP(hqsl, kc * 256 + tt * 128, [[22 * 256, 128], [1, 128]]),
                                     tsp)
        P.y.wait(tsp, tsp.n)
        for kc in range(22):
            P.y.dma_start(out=hqo[kc * 128:(kc + 1) * 128, t0:t0 + 256],
                          in_=bass.AP(hqsl, kc * 256, [[22 * 256, 128], [1, 256]])).then_inc(stq, 16)
    P.y.wait(stq, stq.n)
    return P


def build_l0dn():
    P = Prog("l0dn")
    bk = P.bank(8)
    hqi = P.inp("hq", [2816, TOK], BF16)
    wdq = P.inp("wdq_st", [8, 2816, 128], BF16)
    g0o = P.out("g0", [1024, TOK], BF16)
    hq = P.sb("hqt", [128, 22 * TOK], BF16)
    ld = P.sem("ld")
    load_fm(P, hq, hqi, 2816, TOK, ld, nch=22)
    P.t.wait(ld, ld.n)
    P.v.wait(ld, ld.n)
    P.s.wait(ld, ld.n)
    poolM = WPool(P, "m", 22 * 128, nbuf=2)
    eg = EvacStore(P, "dn", g0o, TOK, func=AF.Silu, dtype=BF16, engine='s')
    emit_gemm(P, "dn", poolM, wdq, 2816, 1024, chunk_viewer(hq, 22, TOK), TOK, eg,
              nsets=2, banks=bk)
    eg.flush()
    return P


def emit_norm_store(P, uid, attn_un, tokens, onesK1, pb, out_hbm, row0, outt, rrow, rb,
                    stt=None):
    nrm = P.sem(f"nrm_{uid}")
    stt = stt if stt is not None else P.sem(f"nst_{uid}")
    P.v.wait(stt, stt.n)
    P.v.reciprocal(rrow[0:1, 0:tokens], attn_un[64:65, 0:tokens])
    P.vfence(nrm, 1)
    for tb in range(tokens // 512):
        P.t.wait(nrm, nrm.n)
        P.t.matmul(pb[0:64, :], onesK1[0:1, 0:64], rrow[0:1, tb * 512:(tb + 1) * 512],
                   start=True, stop=True).then_inc(nrm, 1)
        P.v.wait(nrm, nrm.n)
        P.v.tensor_copy(rb[0:64, tb * 512:(tb + 1) * 512], pb[0:64, :]).then_inc(nrm, 1)
    P.v.tensor_tensor(outt[0:64, 0:tokens], attn_un[0:64, 0:tokens], rb[0:64, 0:tokens],
                      ALU.mult)
    P.vfence(nrm, 1)
    P.y.wait(nrm, nrm.n)
    P.y.dma_start(out=out_hbm[row0:row0 + 64, 0:tokens], in_=outt[0:64, 0:tokens]).then_inc(stt, 16)
    return stt


def build_l0attnA():
    P = Prog("l0attnA")
    qf = P.inp("qf", [1024, TOK], BF16)
    kh = P.inp("kh", [1024, TOK + 256], BF16)
    vt = P.inp("vt", [TOK + 256, 16 * 65], BF16)
    gq = P.inp("gq", [512, TOK], BF16)
    gkl = P.inp("gkl", [512, 64], F32)
    gvt = P.inp("gvt", [64, 8 * 65], F32)
    lmask = P.inp("lmask", [24 * 128, 512], BF16)
    latm = P.inp("latm", [64, TOK], BF16)
    ones_row = P.inp("ones_row", [1, 128], F32)
    loco = P.out("loc", [1024, TOK], BF16)
    remo = P.out("rem", [512, TOK], BF16)

    KTOT = TOK + 256
    qt = P.sb("qt", [128, 8 * TOK], BF16)
    kht = P.sb("kht", [128, 8 * KTOT], BF16)
    vtt = P.sb("vtt", [128, 18 * 1040], BF16)
    gqt = P.sb("gqt", [128, 4 * TOK], BF16)
    gklt = P.sb("gklt", [128, 4 * 64], BF16)
    gvtt = P.sb("gvtt", [64, 520], BF16)
    lmt = P.sb("lmt", [128, 24 * 512], BF16)
    latmt = P.sb("latmt", [64, TOK], BF16)
    onesK1 = P.sb("onesK1", [1, 128], F32)
    pT = P.sb("pT", [128, 2 * 1536], BF16)
    attn_un = P.sb("attn_un", [65, TOK], F32)
    rrow = P.sb("rrow", [1, TOK], F32)
    rb = P.sb("rb", [64, TOK], F32)
    outt = P.sb("outt", [64, TOK], BF16)
    sc = [P.ps("sc0", [128, 1536], F32), P.ps("sc1", [128, 1536], F32)]
    pv = P.ps("pv", [128, 512], F32)
    pb = P.ps("pb", [128, 512], F32)

    ld = P.sem("ld")
    ldg = P.sem("ldg")
    P.y.dma_start(out=onesK1[:, :], in_=ones_row[:, :]).then_inc(ld, 16)
    load_fm(P, qt, qf, 1024, TOK, ld)
    load_fm(P, kht, kh, 1024, KTOT, ld)
    P.y.dma_start(out=bass.AP(vtt, 0, [[18 * 1040, 128], [1040, 18], [1, 1040]]),
                  in_=vt.rearrange("(t p) c -> p t c", p=128)).then_inc(ld, 16)
    load_fm(P, gqt, gq, 512, TOK, ld)
    P.g.dma_start(out=bass.AP(gklt, 0, [[4 * 64, 128], [64, 4], [1, 64]]),
                  in_=gkl.rearrange("(c p) l -> p c l", p=128)).then_inc(ldg, 16)
    P.g.dma_start(out=gvtt[0:64, :], in_=gvt[:, :]).then_inc(ldg, 16)
    P.y.dma_start(out=bass.AP(lmt, 0, [[24 * 512, 128], [512, 24], [1, 512]]),
                  in_=lmask.rearrange("(m p) c -> p m c", p=128)).then_inc(ld, 16)
    P.y.dma_start(out=latmt[0:64, :], in_=latm[:, :]).then_inc(ld, 16)
    for e in (P.t, P.v, P.s):
        e.wait(ld, ld.n)
        e.wait(ldg, ldg.n)

    mmsc = P.sem("mmsc"); exd = P.sem("exd"); mskd = P.sem("mskd")
    pvd = P.sem("pvd"); evd = P.sem("evd")
    sttsh = P.sem("sttsh")
    allst = []
    NG = 3
    gseq = 0
    for h in range(16):
        ch, po = h // 2, (h % 2) * 64
        for qb in range(4):
            q0 = qb * 512
            def emit_scores(g):
                n = gseq + g
                par = n % 2
                if n >= 2:
                    P.t.wait(exd, n - 1)
                rec = None
                for j in range(3):
                    kt_i = g * 3 + j
                    klhs = kht[po:po + 64, ch * KTOT + q0 + kt_i * 128:
                               ch * KTOT + q0 + kt_i * 128 + 128]
                    rec = P.t.matmul(sc[par][:, j * 512:(j + 1) * 512],
                                     klhs, qt[po:po + 64, ch * TOK + q0: ch * TOK + q0 + 512],
                                     start=True, stop=True)
                rec.then_inc(mmsc, 1)
                P.s.wait(mmsc, n + 1)
                if n >= 2:
                    P.s.wait(pvd, n - 1)
                P.s.activation(pT[:, par * 1536: par * 1536 + 1536],
                               sc[par][:, :], AF.Exp, scale=1.0 / 8).then_inc(exd, 1)
                P.v.wait(exd, n + 1)
                lastm = None
                for j in range(3):
                    kt_i = g * 3 + j
                    m = qb * 6 + kt_i
                    seg = pT[0:128, par * 1536 + j * 512: par * 1536 + (j + 1) * 512]
                    lastm = P.v.tensor_tensor(seg, seg,
                                              lmt[0:128, m * 512:(m + 1) * 512], ALU.mult)
                lastm.then_inc(mskd, 1)
                return mskd.n

            def emit_pv(g, gate):
                n = gseq + g
                par = n % 2
                P.t.wait(mskd, gate)
                rec = None
                for j in range(3):
                    kt_i = g * 3 + j
                    ktg = qb * 4 + kt_i
                    vlhs = vtt[0:128, ktg * 1040 + h * 65: ktg * 1040 + (h + 1) * 65]
                    rec = P.t.matmul(pv[0:65, :], vlhs,
                                     pT[0:128, par * 1536 + j * 512: par * 1536 + (j + 1) * 512],
                                     start=(g == 0 and j == 0), stop=(g == 1 and j == 2))
                rec.then_inc(pvd, 1)

            P.t.wait(evd, evd.n)
            g0gate = emit_scores(0)
            g1gate = emit_scores(1)
            emit_pv(0, g0gate)
            emit_pv(1, g1gate)
            gseq += 2
            P.v.wait(pvd, pvd.n)
            P.v.tensor_copy(attn_un[0:65, q0:q0 + 512], pv[0:65, :]).then_inc(evd, 1)
            P.t.wait(evd, evd.n)
        allst.append(emit_norm_store(P, f"h{h}", attn_un, TOK, onesK1, pb, loco, h * 64,
                                     outt, rrow, rb, stt=sttsh))
    for lh in range(8):
        ch, po = lh // 2, (lh % 2) * 64
        for qb in range(4):
            q0 = qb * 512
            P.t.wait(evd, evd.n)
            if gseq >= 2:
                P.t.wait(exd, gseq - 1)
            par = gseq % 2
            P.t.matmul(sc[par][0:64, 0:512],
                       gklt[po:po + 64, ch * 64: ch * 64 + 64],
                       gqt[po:po + 64, ch * TOK + q0: ch * TOK + q0 + 512],
                       start=True, stop=True).then_inc(mmsc, 1)
            P.s.wait(mmsc, mmsc.n)
            if gseq >= 2:
                P.s.wait(pvd, gseq - 1)
            P.s.activation(pT[0:64, par * 1536: par * 1536 + 512], sc[par][0:64, 0:512],
                           AF.Exp, scale=1.0 / 8).then_inc(exd, 1)
            P.v.wait(exd, exd.n)
            P.v.tensor_tensor(pT[0:64, par * 1536: par * 1536 + 512],
                              pT[0:64, par * 1536: par * 1536 + 512],
                              latmt[0:64, q0:q0 + 512], ALU.mult).then_inc(mskd, 1)
            P.t.wait(mskd, mskd.n)
            P.t.matmul(pv[0:65, :], gvtt[0:64, lh * 65:(lh + 1) * 65],
                       pT[0:64, par * 1536: par * 1536 + 512],
                       start=True, stop=True).then_inc(pvd, 1)
            gseq += 1
            P.v.wait(pvd, pvd.n)
            P.v.tensor_copy(attn_un[0:65, q0:q0 + 512], pv[0:65, :]).then_inc(evd, 1)
            P.t.wait(evd, evd.n)
        allst.append(emit_norm_store(P, f"l{lh}", attn_un, TOK, onesK1, pb, remo, lh * 64,
                                     outt, rrow, rb, stt=sttsh))
    for stt in allst:
        P.y.wait(stt, stt.n)
    return P


def build_l0attnB():
    P = Prog("l0attnB")
    bk = P.bank(8)
    loc = P.inp("loc", [1024, TOK], BF16)
    rem = P.inp("rem", [512, TOK], BF16)
    gl = P.inp("gl", [16, TOK], F32)
    gr = P.inp("gr", [16, TOK], F32)
    emat = P.inp("emat", [16, 8 * 128], F32)
    rout = P.inp("rout_st", [8, 512, 128], F32)
    wout = P.inp("wout_st", [8, 1024, 128], F32)
    s0o = P.out("s0", [1024, TOK], BF16)

    loct = P.sb("loct", [128, 8 * TOK], BF16)
    remt = P.sb("remt", [128, 4 * TOK], BF16)
    cot = P.sb("cot", [128, 8 * TOK], BF16)
    glt = P.sb("glt", [16, TOK], BF16)
    grt = P.sb("grt", [16, TOK], BF16)
    ematt = P.sb("ematt", [16, 8 * 128], BF16)
    gbl = P.sb("gbl", [128, 512], F32)
    gbr = P.sb("gbr", [128, 512], F32)
    tmp = P.sb("tmpc", [128, 512], F32)

    ld = P.sem("ld"); ldg = P.sem("ldg")
    load_fm(P, loct, loc, 1024, TOK, ld)
    load_fm(P, remt, rem, 512, TOK, ld)
    P.g.dma_start(out=glt[0:16, :], in_=gl[:, :]).then_inc(ldg, 16)
    P.g.dma_start(out=grt[0:16, :], in_=gr[:, :]).then_inc(ldg, 16)
    P.g.dma_start(out=ematt[0:16, :], in_=emat[:, :]).then_inc(ldg, 16)
    for e in (P.t, P.v, P.s):
        e.wait(ld, ld.n)
        e.wait(ldg, ldg.n)

    poolR = WPool(P, "r", 8 * 128, nbuf=2)
    cmbd = P.sem("cmbd")
    gmm = P.sem("gmm")


    def combine(mt, tb, psum_ap, mmsem, mmval):
        P.t.wait(mmsem, mmval)
        if mt + tb > 0:
            P.t.wait(cmbd, cmbd.n)
        P.t.matmul(bk[6][:, :], ematt[0:16, mt * 128:(mt + 1) * 128],
                   glt[0:16, tb * 512:(tb + 1) * 512], start=True, stop=True)
        P.t.matmul(bk[7][:, :], ematt[0:16, mt * 128:(mt + 1) * 128],
                   grt[0:16, tb * 512:(tb + 1) * 512], start=True, stop=True).then_inc(gmm, 1)
        P.v.wait(gmm, gmm.n)
        P.v.wait(mmsem, mmval)
        P.v.tensor_copy(gbr[:, :], bk[7][:, :])
        P.v.tensor_tensor(tmp[:, :], psum_ap, gbr[:, :], ALU.mult)
        P.v.tensor_copy(gbl[:, :], bk[6][:, :])
        dst = chunk_view(cot, 8, TOK, mt, tb * 512, (tb + 1) * 512)
        P.v.tensor_tensor(dst, chunk_view(loct, 8, TOK, mt, tb * 512, (tb + 1) * 512),
                          gbl[:, :], ALU.mult)
        P.v.tensor_tensor(dst, dst, tmp[:, :], ALU.add).then_inc(cmbd, 1)
        return (cmbd, cmbd.n)
    emit_gemm(P, "ro", poolR, rout, 512, 1024, chunk_viewer(remt, 4, TOK), TOK,
              combine, nsets=1, banks=bk[0:4])
    P.t.wait(cmbd, cmbd.n)
    es = EvacStore(P, "wo", s0o, TOK, dtype=BF16, engine='v')
    emit_gemm(P, "wo", poolR, wout, 1024, 1024, chunk_viewer(cot, 8, TOK), TOK, es,
              nsets=2, banks=bk[0:8])
    es.flush()
    return P


def l0_proj_maps(inputs, x_fm):
    wq_st = strips(inputs['L0_attn_wq'])
    wk_st = strips(inputs['L0_attn_wk'])
    wv_st = strips(inputs['L0_attn_wv'])
    wgq_st = strips(inputs['L0_attn_wgq'])
    wgk_st = strips(inputs['L0_attn_wgk'])
    wgv_st = strips(inputs['L0_attn_wgv'])
    G2 = np.zeros((1024, 128), np.float32)
    glw, grw = inputs['L0_attn_gate_local_w'], inputs['L0_attn_gate_remote_w']
    for h in range(16):
        G2[h * 64:(h + 1) * 64, h] = glw[h]
        G2[h * 64:(h + 1) * 64, 32 + h] = grw[h]
    g2_st = strips(G2)
    maps = []
    for c in range(NCORES):
        m = host_head_inputs(x_fm[c])
        m.update({"wq_st": wq_st, "wk_st": wk_st, "wv_st": wv_st, "wgq_st": wgq_st,
                  "wgkp_st": wgk_st, "wgvp_st": wgv_st, "g2_st": g2_st})
        maps.append(m)
    return maps


def l0_wq_maps(inputs):
    rows = np.concatenate([inputs['L0_ffn_w_gate'], inputs['L0_ffn_w_up'],
                           inputs['L0_ffn_w_down'].reshape(2816, 1024)], axis=0)
    rows = np.concatenate([rows, np.zeros((9216 - rows.shape[0], 1024), np.float32)], axis=0)
    return [{"wrows": _c(rows[c * 1152:(c + 1) * 1152])} for c in range(NCORES)]


def l0_wq_assemble(wq_parts):
    full = np.concatenate([np.asarray(p) for p in wq_parts], axis=0)[:8448]
    wgq = full[0:2816]
    wuq = full[2816:5632]
    wdq = full[5632:8448].reshape(1024, 2816)
    return (strips(_c(wgq.T.astype(BF))), strips(_c(wuq.T.astype(BF))),
            strips(_c(wdq.T.astype(BF))))


def l0_local_mask(core):
    half = core % 2
    m = np.zeros((24, 128, 512), np.float32)
    for qb in range(4):
        for kt in range(6):
            kp = np.arange(128)[:, None]
            qf = np.arange(512)[None, :]
            dist = 256 + qf - 128 * kt - kp
            valid = (dist >= 0) & (dist <= 255)
            if half == 0:
                kglob = qb * 512 + kt * 128 + kp - 256
                valid = valid & (kglob >= 0)
            m[qb * 6 + kt] = valid
    return m.reshape(24 * 128, 512).astype(BF)


def l0_attnA_maps(projres):
    ones_row = np.ones((1, 128), np.float32)
    maps = []
    for c in range(NCORES):
        b, half = c // 2, c % 2
        k_own = np.asarray(projres[c]['k'])
        v_own = np.asarray(projres[c]['v'])
        if half == 0:
            kprev = np.zeros((1024, 256), dtype=k_own.dtype)
            vprev = np.zeros((1024, 256), dtype=v_own.dtype)
        else:
            kprev = np.asarray(projres[c - 1]['k'])[:, -256:]
            vprev = np.asarray(projres[c - 1]['v'])[:, -256:]
        kh = np.concatenate([kprev, k_own], axis=1)
        vh = np.concatenate([vprev, v_own], axis=1)
        vT = vh.T
        vt = np.zeros((TOK + 256, 16, 65), dtype=BF)
        for j in range(16):
            vt[:, j, 0:64] = vT[:, j * 64:(j + 1) * 64]
        vt[:, :, 64] = np.float32(1.0)
        gkl = np.concatenate([np.asarray(projres[2 * b]['gkl']),
                              np.asarray(projres[2 * b + 1]['gkl'])], axis=1)
        gvl = np.concatenate([np.asarray(projres[2 * b]['gvl']),
                              np.asarray(projres[2 * b + 1]['gvl'])], axis=1)
        gvt = np.zeros((64, 8, 65), np.float32)
        for j in range(8):
            gvt[:, j, 0:64] = gvl[j * 64:(j + 1) * 64, :].T
        gvt[:, :, 64] = 1.0
        off = half * TOK
        latm = (np.arange(64)[:, None] <= ((off + np.arange(TOK)) // 64)[None, :]).astype(np.float32)
        maps.append({"qf": np.asarray(projres[c]['q']), "kh": _c(kh),
                     "vt": _c(vt.reshape(TOK + 256, 16 * 65)),
                     "gq": np.asarray(projres[c]['gq']), "gkl": _c(gkl.astype(np.float32)),
                     "gvt": _c(gvt.reshape(64, 520)),
                     "lmask": l0_local_mask(c), "latm": latm.astype(BF),
                     "ones_row": ones_row})
    return maps


def l0_attnB_maps(inputs, projres, ares):
    rout_st = strips(inputs['L0_attn_r_out_w'])
    wout_st = strips(inputs['L0_attn_w_out'])
    emat = np.zeros((16, 8, 128), np.float32)
    for f in range(1024):
        emat[f // 64, f // 128, f % 128] = 1.0
    emat = emat.reshape(16, 8 * 128)
    maps = []
    for c in range(NCORES):
        maps.append({"loc": np.asarray(ares[c]['loc']), "rem": np.asarray(ares[c]['rem']),
                     "gl": np.asarray(projres[c]['gl']), "gr": np.asarray(projres[c]['gr']),
                     "emat": emat, "rout_st": rout_st, "wout_st": wout_st})
    return maps


IDENT128 = np.eye(128, dtype=np.float32)


def build_l2qkv():
    P = Prog("l2qkv")
    bankpool = P.bank(8)
    P._ln_banks = bankpool[0:4]
    xt, xn, ln, done, st, cstt = emit_loadx_ln(P)
    wq = P.inp("wqkv", [1024, 3072], F32)
    lq1 = P.inp("lq1", [128, 2048], F32)
    lic = P.inp("lic", [128, 1], F32)
    qkvo = P.out("qkv", [TOK, 3072], BF16)
    nlamo = P.out("nlam", [128, 8], F32)
    LIc = 0.8 - 0.6 * float(np.exp(-0.3))
    lqrep = P.sb("lqrep", [128, 2048], F32)
    lprod = P.sb("lprod", [128, 1024], F32)
    lsc = P.sb("lsc", [128, 8], F32)
    lmm = P.sem("lmm")
    lld = P.sem("lld")
    lict = P.sb("lict", [128, 1], F32)
    P.y.dma_start(out=lict[:, :], in_=lic[:, :]).then_inc(lld, 16)
    P.y.dma_start(out=lqrep[:, :], in_=lq1[:, :]).then_inc(lld, 16)
    P.v.wait(lld, lld.n)
    P.v.tensor_tensor(lprod[:, 0:512], lqrep[:, 0:512], lqrep[:, 512:1024], ALU.mult)
    P.v.tensor_tensor(lprod[:, 512:1024], lqrep[:, 1024:1536], lqrep[:, 1536:2048], ALU.mult)
    P.v.reduce_sum(bass.AP(lsc, 0, [[8, 128], [1, 1]]), lprod[:, 0:512], AX.X)
    P.v.reduce_sum(bass.AP(lsc, 1, [[8, 128], [1, 1]]), lprod[:, 512:1024], AX.X)
    P.vfence(lmm, 1)
    P.s.wait(lmm, lmm.n)
    P.s.activation(lsc[:, 2:4], lsc[:, 0:2], AF.Exp)
    P.sfence(lmm, 1)
    P.v.wait(lmm, lmm.n)
    P.v.tensor_sub(lsc[:, 4:5], lsc[:, 3:4], lsc[:, 2:3])
    P.v.tensor_single_scalar(lsc[:, 5:6], lsc[:, 4:5], lict[:, 0:1], ALU.subtract)
    P.vfence(lmm, 1)
    wsb = P.sb("wsb", [128, 8 * 3072], BF16)
    ot = [P.sb(f"ot{i}", [128, 3072], BF16) for i in range(2)]
    ldw = P.sem("ldw")
    for kc in range(NCH):
        P.g.dma_start(out=bass.AP(wsb, kc * 3072, [[8 * 3072, 128], [1, 3072]]),
                      in_=wq[kc * 128:(kc + 1) * 128, :]).then_inc(ldw, 16)
    P.t.wait(done, done.n)
    P.t.wait(ldw, ldw.n)
    mmq = P.sem("mmq")
    evq = P.sem("evq")
    stq = [P.sem(f"stq{i}") for i in range(2)]
    uses = [0, 0]
    ps = bankpool[0:6]
    for tt in range(16):
        if tt >= 1:
            P.t.wait(evq, tt * 7)
        rec = None
        for kc in range(NCH):
            lhs = chunk_view(xn, NCH, TOK, kc, tt * 128, (tt + 1) * 128)
            for nb in range(6):
                rec = P.t.matmul(ps[nb][:, :], lhs, wsb[:, kc * 3072 + nb * 512:
                                                        kc * 3072 + (nb + 1) * 512],
                                 start=(kc == 0), stop=(kc == NCH - 1))
        rec.then_inc(mmq, 1)
        par = tt % 2
        P.v.wait(mmq, mmq.n)
        if uses[par] > 0:
            P.v.wait(stq[par], 16 * uses[par])
        for nb in range(6):
            P.v.tensor_copy(ot[par][:, nb * 512:(nb + 1) * 512], ps[nb][:, :]).then_inc(evq, 1)
        P.vfence(evq, 1)
        P.y.wait(evq, evq.n)
        P.y.dma_start(out=qkvo[tt * 128:(tt + 1) * 128, :], in_=ot[par][:, :]).then_inc(stq[par], 16)
        uses[par] += 1
    P.y.wait(stq[0], stq[0].n)
    P.y.wait(stq[1], stq[1].n)
    stl = P.sem("stl")
    P.y.wait(lmm, lmm.n)
    P.y.dma_start(out=nlamo[:, :], in_=lsc[:, :]).then_inc(stl, 16)
    P.y.wait(stl, stl.n)
    P.y.wait(P._sems["st_head"], P._sems["st_head"].n)
    return P


def build_l2attn():
    P = Prog("l2attn")
    SQ = 4096
    LI = 0.8 - 0.6 * float(np.exp(-0.3))
    qkv = P.inp("qkv", [TOK, 3072], BF16)
    nlam = P.inp("nlam", [128, 8], F32)
    li1m = P.inp("li1m", [128, 1], F32)
    ctpl = P.inp("ctpl", [128, 896], BF16)
    idb = P.inp("idb", [128, 128], F32)
    ones_row = P.inp("ones_row", [1, 128], F32)
    out = P.out("attnh", [512, SQ], BF16)
    import os as _os
    _dbg = _os.environ.get("L2DBG") == "1"
    if _dbg:
        kfmo = P.out("kfmo", [64, 4 * SQ], BF16)
        qfmo = P.out("qfmo", [64, 2 * SQ], BF16)
        auno = P.out("auno", [65, 2 * SQ], F32)
        rrowo = P.out("rrowo", [1, 2 * SQ], F32)
        lamo = P.out("lamo", [64, 1], F32)
        o0o = P.out("o0o", [64, SQ], F32)
        rbo = P.out("rbo", [64, SQ], F32)

    ktm = P.sb("ktm", [128, 4 * 32 * 64], BF16)
    vtm = P.sb("vtm", [128, 4 * 32 * 65], BF16)
    kfm = P.sb("kfm", [64, 4 * SQ], BF16)
    qfm = P.sb("qfm", [64, 2 * SQ], BF16)
    idbt = P.sb("idbt", [128, 128], BF16)
    tplt = P.sb("tplt", [128, 896], BF16)
    onesK1 = P.sb("onesK1", [1, 128], F32)
    lam64 = P.sb("lam64", [128, 8], F32)
    li1mt = P.sb("li1mt", [128, 1], F32)
    pT = P.sb("pT", [128, 4 * 1024], BF16)
    aun = P.sb("aun", [65, 2 * SQ], F32)
    rrow = P.sb("rrow", [1, 2 * SQ], F32)
    rb = P.sb("rb", [64, SQ], F32)
    o0 = P.sb("o0", [64, SQ], F32)
    outt = P.sb("outt", [64, SQ], BF16)
    fence = P.sb("fence", [1, 4], F32)
    fence_s = P.sb("fence_s", [1, 4], F32)
    onesb = P.sb("onesb", [128, 1], BF16)
    sc = [P.ps("sc0", [128, 1024], F32), P.ps("sc1", [128, 1024], F32)]
    pv = P.ps("pv", [128, 512], F32)
    pv2 = P.ps("pv2", [128, 512], F32)
    pb = P.ps("pb", [128, 512], F32)
    ptb = P.ps("ptb", [128, 1024], BF16)

    core_half = None

    ld = P.sem("ld"); ldg = P.sem("ldg")
    P.y.dma_start(out=tplt[:, :], in_=ctpl[:, :]).then_inc(ld, 16)
    P.y.dma_start(out=onesK1[:, :], in_=ones_row[:, :]).then_inc(ld, 16)
    P.g.dma_start(out=idbt[:, :], in_=idb[:, :]).then_inc(ldg, 16)
    P.g.dma_start(out=lam64[:, 0:8], in_=nlam[:, :]).then_inc(ldg, 16)
    P.g.dma_start(out=li1mt[:, :], in_=li1m[:, :]).then_inc(ldg, 16)
    qkvh = qkv.tensor
    for jk in range(4):
        for t in range(32):
            s2 = bass.AP(qkvh, (jk * 512 + t * 16) * 3072 + 2048,
                         [[3072, 16], [64, 8], [1, 64]])
            P.y.dma_start(out=bass.AP(ktm, (jk * 32 + t) * 64,
                                      [[4 * 32 * 64, 128], [1, 64]]),
                          in_=s2).then_inc(ld, 16)
            s3 = bass.AP(qkvh, (jk * 512 + t * 16) * 3072 + 2560,
                         [[3072, 16], [64, 8], [1, 64]])
            P.y.dma_start(out=bass.AP(vtm, (jk * 32 + t) * 65,
                                      [[4 * 32 * 65, 128], [1, 64]]),
                          in_=s3).then_inc(ld, 16)
    P.v.wait(ld, ld.n)
    P.v.memset(onesb[:, :], 1.0)
    vmd = P.sem("vmd")
    P.v.nop().then_inc(vmd, 1)
    lmm = P.sem("lmm")
    tsp = P.sem("tsp")
    P.t.wait(ld, ld.n)
    for jk in range(4):
        for t in range(32):
            emit_transpose_block(P, ptb[0:64, 0:128], idbt[:, :],
                                 bass.AP(ktm, (jk * 32 + t) * 64, [[4 * 32 * 64, 128], [1, 64]]),
                                 bass.AP(kfm, jk * SQ + t * 128, [[4 * SQ, 64], [1, 128]]),
                                 tsp)

    if _dbg:
        P.vfence(lmm, 1)
        P.y.wait(lmm, lmm.n)
        lst = P.sem("lst")
        P.y.dma_start(out=lamo[0:64, 0:1], in_=lam64[0:64, 0:1], allow_slow_non_contiguous=True).then_inc(lst, 16)
        P.y.wait(lst, lst.n)

    mmsc = P.sem("mmsc"); exd = P.sem("exd"); mskd = P.sem("mskd")
    pvd = P.sem("pvd"); evd = P.sem("evd"); nrm = P.sem("nrm"); stt = P.sem("stt")
    if _dbg:
        P.y.wait(tsp, tsp.n)
        P.y.dma_start(out=kfmo[0:64, :], in_=kfm[0:64, :]).then_inc(stt, 16)
    NG = 2
    gseq = 0
    for oh in range(8):
        for ji, j in enumerate((2 * oh, 2 * oh + 1)):
            for t in range(32):
                s2 = bass.AP(qkvh, (j * 128 + t * 4) * 3072,
                             [[3072, 4], [64, 32], [1, 64]])
                P.y.wait(tsp, tsp.n)
                P.y.dma_start(out=bass.AP(ktm, 0, [[4 * 32 * 64, 128], [1, 64]]),
                              in_=s2).then_inc(ld, 16)
                P.t.wait(ld, ld.n)
                emit_transpose_block(P, ptb[0:64, 0:128], idbt[:, :],
                                     bass.AP(ktm, 0, [[4 * 32 * 64, 128], [1, 64]]),
                                     bass.AP(qfm, ji * SQ + t * 128, [[2 * SQ, 64], [1, 128]]),
                                     tsp)
        if _dbg and oh == 0:
            P.y.wait(tsp, tsp.n)
            P.y.dma_start(out=qfmo[0:64, :], in_=qfm[0:64, :]).then_inc(stt, 16)
        jk = oh // 2
        for ji in range(2):
            for qb in range(8):
                q0 = qb * 512
                nkt = 4 * qb + 4
                ngrp = (nkt + NG - 1) // NG
                qrhs = bass.AP(qfm, ji * SQ + q0, [[2 * SQ, 64], [1, 512]])

                def emit_scores(g):
                    n = gseq + g
                    par = n % 2
                    par4 = n % 4
                    if n >= 2:
                        P.t.wait(exd, n - 1)
                    rec = None
                    for jj in range(min(NG, nkt - g * NG)):
                        kt_i = g * NG + jj
                        klhs = bass.AP(kfm, jk * SQ + kt_i * 128, [[4 * SQ, 64], [1, 128]])
                        rec = P.t.matmul(sc[par][:, jj * 512:(jj + 1) * 512], klhs, qrhs,
                                         start=True, stop=True)
                    rec.then_inc(mmsc, 1)
                    n_in_g = min(NG, nkt - g * NG)
                    P.s.wait(mmsc, mmsc.n)
                    if n >= 4:
                        P.s.wait(pvd, n - 3)
                    P.s.activation(pT[:, par4 * 1024: par4 * 1024 + n_in_g * 512],
                                   sc[par][:, 0:n_in_g * 512], AF.Exp,
                                   scale=1.0 / 8)
                    P.s.activation(fence_s[0:1, 0:1], pT[0:1, par4 * 1024: par4 * 1024 + 1],
                                   AF.Identity).then_inc(exd, 1)
                    lastm = None
                    for jj in range(n_in_g):
                        kt_i = g * NG + jj
                        if 4 * qb <= kt_i:
                            d0 = kt_i * 128 - q0
                            if lastm is None:
                                P.v.wait(exd, n + 1)
                            seg = pT[0:128, par4 * 1024 + jj * 512: par4 * 1024 + (jj + 1) * 512]
                            lastm = P.v.tensor_tensor(seg, seg,
                                                      tplt[:, 384 - d0:384 - d0 + 512], ALU.mult)
                    if lastm is None:
                        P.v.wait(exd, n + 1)
                    P.v.tensor_copy(fence[0:1, 0:1],
                                    pT[0:1, par4 * 1024: par4 * 1024 + 1]).then_inc(mskd, 1)
                    return ('m', mskd.n)

                def emit_pv(g, gate):
                    n = gseq + g
                    par4 = n % 4
                    if gate[0] == 'm':
                        P.t.wait(mskd, gate[1])
                    else:
                        P.t.wait(exd, gate[1])
                    rec = None
                    for jj in range(min(NG, nkt - g * NG)):
                        kt_i = g * NG + jj
                        vlhs = bass.AP(vtm, (jk * 32 + kt_i) * 65, [[4 * 32 * 65, 128], [1, 64]])
                        seg = pT[0:128, par4 * 1024 + jj * 512: par4 * 1024 + (jj + 1) * 512]
                        P.t.matmul(pv[0:64, :], vlhs, seg,
                                   start=(kt_i == 0), stop=(kt_i == nkt - 1))
                        rec = P.t.matmul(pv2[0:1, :], onesb[:, :], seg,
                                         start=(kt_i == 0), stop=(kt_i == nkt - 1))
                    rec.then_inc(pvd, 1)

                if gseq == 0:
                    P.t.wait(vmd, vmd.n)
                P.t.wait(evd, evd.n)
                import os as _os2
                if _os2.environ.get("L2SER") == "1":
                    for g in range(ngrp):
                        gt = emit_scores(g)
                        emit_pv(g, gt)
                else:
                    gates = {}
                    for g in range(ngrp):
                        gates[g] = emit_scores(g)
                        if g >= 1:
                            emit_pv(g - 1, gates[g - 1])
                    emit_pv(ngrp - 1, gates[ngrp - 1])
                gseq += ngrp
                P.v.wait(pvd, pvd.n)
                P.v.tensor_copy(aun[0:64, ji * SQ + q0: ji * SQ + q0 + 512], pv[0:64, :])
                P.v.tensor_copy(aun[64:65, ji * SQ + q0: ji * SQ + q0 + 512],
                                pv2[0:1, :]).then_inc(evd, 1)
                P.t.wait(evd, evd.n)
        if _dbg and oh == 0:
            P.y.wait(evd, evd.n)
            P.y.dma_start(out=auno[0:65, :], in_=aun[0:65, :]).then_inc(stt, 16)
        P.v.wait(stt, stt.n)
        P.v.reciprocal(rrow[0:1, :], aun[64:65, :])
        P.vfence(nrm, 1)
        for ji in range(2):
            for tb in range(8):
                P.t.wait(nrm, nrm.n)
                P.t.matmul(pb[0:64, :], onesK1[0:1, 0:64],
                           rrow[0:1, ji * SQ + tb * 512: ji * SQ + tb * 512 + 512],
                           start=True, stop=True).then_inc(nrm, 1)
                P.v.wait(nrm, nrm.n)
                P.v.tensor_copy(rb[0:64, tb * 512:(tb + 1) * 512], pb[0:64, :]).then_inc(nrm, 1)
            if ji == 0:
                P.v.tensor_tensor(o0[0:64, :], aun[0:64, 0:SQ], rb[0:64, :], ALU.mult)
                if _dbg and oh == 0:
                    P.vfence(nrm, 1)
                    P.y.wait(nrm, nrm.n)
                    P.y.dma_start(out=rrowo[0:1, :], in_=rrow[0:1, :]).then_inc(stt, 16)
                    P.y.dma_start(out=o0o[0:64, :], in_=o0[0:64, :]).then_inc(stt, 16)
                    P.v.wait(stt, stt.n)
                P.v.nop().then_inc(nrm, 1)
            else:
                P.v.tensor_tensor(rb[0:64, :], aun[0:64, SQ:2 * SQ], rb[0:64, :], ALU.mult)
                if _dbg and oh == 0:
                    P.vfence(nrm, 1)
                    P.y.wait(nrm, nrm.n)
                    P.y.dma_start(out=rbo[0:64, :], in_=rb[0:64, :]).then_inc(stt, 16)
                    P.v.wait(stt, stt.n)
                P.v.scalar_tensor_tensor(o0[0:64, :], rb[0:64, :], lam64[0:64, 5:6],
                                         o0[0:64, :], ALU.mult, ALU.add)
                P.v.tensor_single_scalar(outt[0:64, :], o0[0:64, :], li1mt[0:64, 0:1],
                                         ALU.mult)
                P.vfence(nrm, 1)
        P.y.wait(nrm, nrm.n)
        P.y.dma_start(out=out[oh * 64:(oh + 1) * 64, :], in_=outt[0:64, :]).then_inc(stt, 16)
    P.y.wait(stt, stt.n)
    return P


def build_l2gr():
    P = Prog("l2gr")
    bk = P.bank(8)
    attn = P.inp("attn", [1024, TOK], BF16)
    wproj = P.inp("wproj", [1024, 1024], F32)
    g2o = P.out("g2", [2 * 1024, 1024], BF16)
    att = P.sb("att", [128, 8 * TOK], BF16)
    wp = P.sb("wp", [128, 8 * 1024], BF16)
    hr = P.sb("hr", [128, 8 * 1024], BF16)
    ot = [P.sb(f"ot{i}", [128, 1024], BF16) for i in range(2)]
    ld = P.sem("ld"); ldg = P.sem("ldg")
    load_fm(P, att, attn, 1024, TOK, ld)
    for kc in range(8):
        P.g.dma_start(out=bass.AP(wp, kc * 1024, [[8 * 1024, 128], [1, 1024]]),
                      in_=wproj[kc * 128:(kc + 1) * 128, :]).then_inc(ldg, 16)
    P.t.wait(ld, ld.n)
    P.t.wait(ldg, ldg.n)
    mmh = P.sem("mmh"); evh = P.sem("evh")
    sts = [P.sem(f"st{i}") for i in range(2)]
    uses = [0, 0]
    for r in range(2):
        for cb in range(8):
            if True:
                P.t.wait(evh, evh.n)
            rec = None
            for kc in range(8):
                lhs = chunk_view(att, 8, TOK, kc, r * 1024 + cb * 128, r * 1024 + (cb + 1) * 128)
                for ab in range(2):
                    rec = P.t.matmul(bk[ab][:, :], lhs,
                                     bass.AP(wp, kc * 1024 + ab * 512, [[8 * 1024, 128], [1, 512]]),
                                     start=(kc == 0), stop=(kc == 7))
            rec.then_inc(mmh, 1)
            P.v.wait(mmh, mmh.n)
            for ab in range(2):
                P.v.tensor_copy(bass.AP(hr, cb * 1024 + ab * 512, [[8 * 1024, 128], [1, 512]]),
                                bk[ab][:, :]).then_inc(evh, 1)
        for mb in range(8):
            P.t.wait(evh, evh.n)
            rec = None
            for cc in range(8):
                lhs = bass.AP(wp, cc * 1024 + mb * 128, [[8 * 1024, 128], [1, 128]])
                for ab in range(2):
                    rec = P.t.matmul(bk[2 + ab][:, :], lhs,
                                     bass.AP(hr, cc * 1024 + ab * 512, [[8 * 1024, 128], [1, 512]]),
                                     start=(cc == 0), stop=(cc == 7))
            rec.then_inc(mmh, 1)
            par = mb % 2
            P.v.wait(mmh, mmh.n)
            if uses[par] > 0:
                P.v.wait(sts[par], 16 * uses[par])
            for ab in range(2):
                P.v.tensor_copy(ot[par][:, ab * 512:(ab + 1) * 512], bk[2 + ab][:, :]).then_inc(evh, 1)
            P.vfence(evh, 1)
            P.y.wait(evh, evh.n)
            P.y.dma_start(out=g2o[r * 1024 + mb * 128: r * 1024 + (mb + 1) * 128, :],
                          in_=ot[par][:, :]).then_inc(sts[par], 16)
            uses[par] += 1
    P.y.wait(sts[0], sts[0].n)
    P.y.wait(sts[1], sts[1].n)
    return P


def l2_qkv_maps(inputs, x_fm):
    lq = np.concatenate([inputs['L2_attn_lambda_q1'], inputs['L2_attn_lambda_k1'],
                         inputs['L2_attn_lambda_q2'], inputs['L2_attn_lambda_k2']])
    lqrep = _c(np.tile(lq[None, :].astype(np.float32), (128, 1)))
    maps = []
    for c in range(NCORES):
        m = host_head_inputs(x_fm[c])
        m["wqkv"] = inputs['L2_attn_Wqkv']
        m["lq1"] = lqrep
        m["lic"] = np.full((128, 1), 0.8 - 0.6 * float(np.exp(-0.3)), np.float32)
        maps.append(m)
    return maps


def l2_attn_maps(inputs, qkvres):
    maps = []
    for c in range(NCORES):
        maps.append({"qkv": np.asarray(qkvres[c]['qkv']),
                     "nlam": np.asarray(qkvres[c]['nlam']),
                     "li1m": np.full((128, 1), 1.0 - (0.8 - 0.6 * float(np.exp(-0.3))), np.float32),
                     "ctpl": causal_tpl(), "idb": IDENT128,
                     "ones_row": np.ones((1, 128), np.float32)})
    return maps


def l2_gr_maps(inputs, ares):
    maps = []
    for c in range(NCORES):
        b, half = c // 2, c % 2
        afull = np.concatenate([np.asarray(ares[2 * b]['attnh']),
                                np.asarray(ares[2 * b + 1]['attnh'])], axis=0)
        attn = _c(afull[:, half * TOK:(half + 1) * TOK])
        maps.append({"attn": attn, "wproj": inputs['L2_attn_Wproj']})
    return maps


def l2_s2_assemble(inputs, grres):
    s2 = np.zeros((B, S, D), np.float32)
    for b in range(B):
        G = np.concatenate([np.asarray(grres[2 * b]['g2']).reshape(2, 1024, 1024),
                            np.asarray(grres[2 * b + 1]['g2']).reshape(2, 1024, 1024)],
                           axis=0).astype(np.float32)
        s2[b] = G.transpose(2, 0, 1).reshape(4096, 1024)
    return s2


TOK1 = TOK + 1


def build_ifA1(heads, name):
    I = heads * 64
    NCI = I // 128
    NQ = 3 * I // 128
    P = Prog(name)
    bk = P.bank(7)
    xh = P.inp("xh", [D, TOK1], F32)
    wst = P.inp("wqkv_st", [NQ, D, 128], F32)
    tabs = P.inp("tabs", [128, 4 * TOK], F32)
    ind2 = P.inp("ind2", [128, 2], BF16)
    bc2 = P.inp("bc2", [2, 128], F32)
    qksc = P.inp("qksc", [128, 2], F32)
    qro = P.out("qr", [I, TOK], BF16)
    kro = P.out("kr", [I, TOK], BF16)
    vvo = P.out("vv", [I, TOK], BF16)

    xe = P.sb("xe", [128, NCH * TOK1], BF16)
    qkvt = P.sb("qkvt", [128, NQ * 1024], BF16)
    tabt = P.sb("tabt", [128, 4 * TOK], F32)
    ind2t = P.sb("ind2t", [128, 2], BF16)
    selat = P.sb("selat", [1, 128], F32)
    selbt = P.sb("selbt", [1, 128], F32)
    qksct = P.sb("qksct", [128, 2], F32)
    sqt = P.sb("sqt", [128, 512], BF16)
    row2 = P.sb("row2", [1, 1024], F32)
    tq = P.sb("tq", [128, 512], F32)
    tqs = P.sb("tqs", [128, 512], F32)
    ot = [P.sb(f"ot{i}", [128, 1024], BF16) for i in range(2)]

    ld = P.sem("ld")
    ldg = P.sem("ldg")
    load_fm(P, xe, xh, D, TOK1, ldg, cast=True)
    P.y.dma_start(out=tabt[:, :], in_=tabs[:, :]).then_inc(ld, 16)
    P.y.dma_start(out=ind2t[:, :], in_=ind2[:, :]).then_inc(ld, 16)
    P.y.dma_start(out=selat[:, :], in_=bc2[0:1, :]).then_inc(ld, 16)
    P.y.dma_start(out=selbt[:, :], in_=bc2[1:2, :]).then_inc(ld, 16)
    P.y.dma_start(out=qksct[:, :], in_=qksc[:, :]).then_inc(ld, 16)
    for e in (P.t, P.v, P.s):
        e.wait(ld, ld.n)
        e.wait(ldg, ldg.n)

    pool = WPool(P, "w", 8 * 128, nbuf=3)
    sqd = P.sem("sqd")
    mma = P.sem("mma")
    sqr = P.sem("sqr")
    vfd = P.sem("vfd")
    bcd = P.sem("bcd")
    vff = P.sem("vff")
    P.v.memset(ot[0][:, :], 0.0)
    P.v.memset(ot[1][:, :], 0.0)
    ost = [P.sem(f"ost{i}") for i in range(2)]
    vst = P.sem("vst")
    uses = [0, 0]
    oseq = 0
    for hf in range(2):
        if hf == 1:
            P.v.wait(vst, vst.n)

        def xv(kc, t0, t1, hf=hf):
            off = kc * TOK1 + (1 if kc < 4 else 0) + hf * 1024 + t0
            return bass.AP(xe, off, [[NCH * TOK1, 128], [1, t1 - t0]])
        ep = EvacPersist(P, f"qkv{hf}", qkvt, 1024, engine='v')
        emit_gemm(P, f"qkv{hf}", pool, wst, D, NQ * 128, xv, 1024, ep,
                  nsets=2, banks=bk[0:4])
        P.s.wait(ep.ev, ep.ev.n)
        P.v.wait(ep.ev, ep.ev.n)
        P.t.wait(ep.ev, ep.ev.n)
        for fam in range(2):
            outhbm = qro if fam == 0 else kro
            tb_cs = fam * 2 * TOK + hf * 1024
            tb_ss = tb_cs + TOK
            for c in range(NCI):
                mt = fam * NCI + c
                slot = oseq % 2
                oseq += 1
                if uses[slot] > 0:
                    P.v.wait(ost[slot], 16 * uses[slot])
                for tb in range(2):
                    src = bass.AP(qkvt, mt * 1024 + tb * 512,
                                  [[NQ * 1024, 128], [1, 512]])
                    P.s.activation(sqt[:, :], src, AF.Square)
                    P.sfence(sqd, 1)
                    P.t.wait(sqd, sqd.n)
                    P.t.matmul(bk[4][0:1, :], ind2t[:, 0:1], sqt[:, :],
                               start=True, stop=True)
                    P.t.matmul(bk[6][0:1, :], ind2t[:, 1:2], sqt[:, :],
                               start=True, stop=True).then_inc(mma, 1)
                    P.s.wait(mma, mma.n)
                    P.s.activation(row2[0:1, 0:512], bk[4][0:1, :], AF.Sqrt)
                    P.s.activation(row2[0:1, 512:1024], bk[6][0:1, :],
                                   AF.Sqrt).then_inc(sqr, 1)
                    P.v.wait(sqr, sqr.n)
                    P.v.tensor_scalar(row2[0:1, :], row2[0:1, :], 1e-12, None, ALU.max)
                    P.v.reciprocal(row2[0:1, :], row2[0:1, :])
                    P.vfence(vfd, 1)
                    P.t.wait(vfd, vfd.n)
                    P.t.matmul(bk[5][:, 0:512], selat[0:1, :], row2[0:1, 0:512],
                               start=True, stop=True)
                    P.t.matmul(bk[6][:, 0:512], selbt[0:1, :], row2[0:1, 512:1024],
                               start=True, stop=True).then_inc(bcd, 1)
                    P.v.wait(bcd, bcd.n)
                    P.v.tensor_tensor(tq[0:64, :], src[0:64, :],
                                      bk[5][0:64, 0:512], ALU.mult)
                    P.v.tensor_tensor(tq[64:128, :], src[64:128, :],
                                      bk[6][64:128, 0:512], ALU.mult)
                    P.v.tensor_single_scalar(tq[:, :], tq[:, :],
                                             qksct[:, fam:fam + 1], ALU.mult)
                    P.v.tensor_copy(tqs[0:32, :], tq[32:64, :])
                    P.v.tensor_copy(tqs[32:64, :], tq[0:32, :])
                    P.v.tensor_copy(tqs[64:96, :], tq[96:128, :])
                    P.v.tensor_copy(tqs[96:128, :], tq[64:96, :])
                    P.v.tensor_tensor(tq[:, :], tq[:, :],
                                      tabt[:, tb_cs + tb * 512: tb_cs + tb * 512 + 512],
                                      ALU.mult)
                    P.v.tensor_tensor(tqs[:, :], tqs[:, :],
                                      tabt[:, tb_ss + tb * 512: tb_ss + tb * 512 + 512],
                                      ALU.mult)
                    P.v.tensor_tensor(ot[slot][:, tb * 512:(tb + 1) * 512],
                                      tq[:, :], tqs[:, :], ALU.add)
                    P.vfence(vff, 1)
                P.y.wait(vff, vff.n)
                P.y.dma_start(out=outhbm[c * 128:(c + 1) * 128, hf * 1024:hf * 1024 + 1024],
                              in_=ot[slot][:, :]).then_inc(ost[slot], 16)
                uses[slot] = (oseq - 1) // 2 + 1
        for c in range(NCI):
            mt = 2 * NCI + c
            P.y.dma_start(out=vvo[c * 128:(c + 1) * 128, hf * 1024:hf * 1024 + 1024],
                          in_=bass.AP(qkvt, mt * 1024, [[NQ * 1024, 128], [1, 1024]])
                          ).then_inc(vst, 16)
    P.y.wait(vst, vst.n)
    P.y.wait(ost[0], ost[0].n)
    P.y.wait(ost[1], ost[1].n)
    return P


def build_ifA2(heads, W, name):
    I = heads * 64
    NCI = I // 128
    KTOT = TOK + W
    nkt = (512 + W) // 128
    ktt = KTOT // 128
    vw = heads * 65
    P = Prog(name)
    qf = P.inp("qf", [I, TOK], BF16)
    kh = P.inp("kh", [I, KTOT], BF16)
    vt = P.inp("vt", [KTOT, vw], BF16)
    lmask = P.inp("lmask", [4 * nkt * 128, 512], BF16)
    ones_row = P.inp("ones_row", [1, 128], F32)
    out = P.out("att", [I, TOK], BF16)

    qt = P.sb("qt", [128, NCI * TOK], BF16)
    kht = P.sb("kht", [128, NCI * KTOT], BF16)
    vtt = P.sb("vtt", [128, ktt * vw], BF16)
    lmt = P.sb("lmt", [128, 4 * nkt * 512], BF16)
    onesK1 = P.sb("onesK1", [1, 128], F32)
    pT = P.sb("pT", [128, 2 * 1536], BF16)
    attn_un = P.sb("attn_un", [65, TOK], F32)
    rrow = P.sb("rrow", [1, TOK], F32)
    rb = P.sb("rb", [64, TOK], F32)
    outt = P.sb("outt", [64, TOK], BF16)
    sc = [P.ps("sc0", [128, 1536], F32), P.ps("sc1", [128, 1536], F32)]
    pv = P.ps("pv", [128, 512], F32)
    pb = P.ps("pb", [128, 512], F32)

    ld = P.sem("ld")
    P.y.dma_start(out=onesK1[:, :], in_=ones_row[:, :]).then_inc(ld, 16)
    load_fm(P, qt, qf, I, TOK, ld)
    load_fm(P, kht, kh, I, KTOT, ld)
    P.y.dma_start(out=bass.AP(vtt, 0, [[ktt * vw, 128], [vw, ktt], [1, vw]]),
                  in_=vt.rearrange("(t p) c -> p t c", p=128)).then_inc(ld, 16)
    P.y.dma_start(out=bass.AP(lmt, 0, [[4 * nkt * 512, 128], [512, 4 * nkt], [1, 512]]),
                  in_=lmask.rearrange("(m p) c -> p m c", p=128)).then_inc(ld, 16)
    for e in (P.t, P.v, P.s):
        e.wait(ld, ld.n)

    mmsc = P.sem("mmsc")
    exd = P.sem("exd")
    vmd = P.sem("vmd")
    pvd = P.sem("pvd")
    evd = P.sem("evd")
    sttsh = P.sem("sttsh")
    allst = []
    NG = 3
    ngrp = (nkt + NG - 1) // NG
    gseq = 0
    for h in range(heads):
        ch, po = h // 2, (h % 2) * 64
        for qb in range(4):
            q0 = qb * 512

            def emit_scores(g):
                n = gseq + g
                par = n % 2
                n_in_g = min(NG, nkt - g * NG)
                if n >= 2:
                    P.t.wait(exd, n - 1)
                rec = None
                for j in range(n_in_g):
                    kt_i = g * NG + j
                    klhs = kht[po:po + 64, ch * KTOT + q0 + kt_i * 128:
                               ch * KTOT + q0 + kt_i * 128 + 128]
                    rec = P.t.matmul(sc[par][:, j * 512:(j + 1) * 512], klhs,
                                     qt[po:po + 64, ch * TOK + q0: ch * TOK + q0 + 512],
                                     start=True, stop=True)
                rec.then_inc(mmsc, 1)
                P.v.wait(mmsc, n + 1)
                for j in range(n_in_g):
                    kt_i = g * NG + j
                    m = qb * nkt + kt_i
                    seg = sc[par][:, j * 512:(j + 1) * 512]
                    P.v.tensor_tensor(seg, seg,
                                      lmt[0:128, m * 512:(m + 1) * 512], ALU.add)
                P.vfence(vmd, 1)
                P.s.wait(vmd, n + 1)
                if n >= 2:
                    P.s.wait(pvd, n - 1)
                P.s.activation(pT[:, par * 1536: par * 1536 + n_in_g * 512],
                               sc[par][:, 0:n_in_g * 512], AF.Exp,
                               scale=1.0).then_inc(exd, 1)
                return exd.n

            def emit_pv(g, gate):
                n = gseq + g
                par = n % 2
                n_in_g = min(NG, nkt - g * NG)
                P.t.wait(exd, gate)
                rec = None
                for j in range(n_in_g):
                    kt_i = g * NG + j
                    ktg = qb * 4 + kt_i
                    vlhs = vtt[0:128, ktg * vw + h * 65: ktg * vw + (h + 1) * 65]
                    rec = P.t.matmul(pv[0:65, :], vlhs,
                                     pT[0:128, par * 1536 + j * 512:
                                        par * 1536 + (j + 1) * 512],
                                     start=(kt_i == 0), stop=(kt_i == nkt - 1))
                rec.then_inc(pvd, 1)

            gates = {}
            P.t.wait(evd, evd.n)
            for g in range(ngrp):
                gates[g] = emit_scores(g)
                if g >= 1:
                    emit_pv(g - 1, gates[g - 1])
            emit_pv(ngrp - 1, gates[ngrp - 1])
            gseq += ngrp
            P.v.wait(pvd, pvd.n)
            P.v.tensor_copy(attn_un[0:65, q0:q0 + 512], pv[0:65, :]).then_inc(evd, 1)
            P.t.wait(evd, evd.n)
        allst.append(emit_norm_store(P, f"h{h}", attn_un, TOK, onesK1, pb, out, h * 64,
                                     outt, rrow, rb, stt=sttsh))
    for stt in allst:
        P.y.wait(stt, stt.n)
    return P


def build_ifA3(heads, name):
    I = heads * 64
    NCI = I // 128
    P = Prog(name)
    bk = P.bank(8)
    o = P.inp("o", [I, TOK], BF16)
    x = P.inp("x", [D, TOK], F32)
    wst = P.inp("wo_st", [8, I, 128], F32)
    xo = P.out("xo", [D, TOK], F32)
    ot = P.sb("ot", [128, NCI * TOK], BF16)
    xt = P.sb("xt", [128, NCH * TOK], F32)
    rt = [P.sb(f"rt{i}", [128, TOK], F32) for i in range(2)]
    ld = P.sem("ld")
    load_fm(P, ot, o, I, TOK, ld)
    load_fm(P, xt, x, D, TOK, ld)
    for e in (P.t, P.v):
        e.wait(ld, ld.n)
    pool = WPool(P, "w", NCI * 128, nbuf=3)
    ev = P.sem("ev")
    st = [P.sem(f"st{i}") for i in range(2)]
    uses = [0, 0]

    def evac(mt, tb, ps_ap, mmsem, mmval):
        slot = mt % 2
        P.v.wait(mmsem, mmval)
        if tb == 0 and uses[slot] > 0:
            P.v.wait(st[slot], 16 * uses[slot])
        P.v.tensor_tensor(rt[slot][:, tb * 512:(tb + 1) * 512], ps_ap,
                          chunk_view(xt, NCH, TOK, mt, tb * 512, (tb + 1) * 512),
                          ALU.add).then_inc(ev, 1)
        if tb == 3:
            P.vfence(ev, 1)
            P.y.wait(ev, ev.n)
            P.y.dma_start(out=xo[mt * 128:(mt + 1) * 128, :],
                          in_=rt[slot][:, :]).then_inc(st[slot], 16)
            uses[slot] += 1
        return (ev, ev.n)

    emit_gemm(P, "wo", pool, wst, I, D, chunk_viewer(ot, NCI, TOK), TOK, evac,
              nsets=2, banks=bk)
    P.y.wait(st[0], st[0].n)
    P.y.wait(st[1], st[1].n)
    return P


def build_ifB():
    P = Prog("ifB")
    bk = P.bank(8)
    P._ln_banks = bk[0:4]
    xh = P.inp("xh", [D, TOK1], F32)
    w1a = P.inp("w1a_st", [22, D, 128], F32)
    w1g = P.inp("w1g_st", [22, D, 128], F32)
    w2 = P.inp("w2_st", [8, 2816, 128], F32)
    lgb = P.inp("lgb", [128, 16], F32)
    bvec = P.inp("bvec", [128, 52], F32)
    one_col = P.inp("one_col", [128, 1], F32)
    eps_col = P.inp("eps_col", [128, 1], F32)
    ones_row = P.inp("ones_row", [1, 128], F32)
    xoo = P.out("xo", [D, TOK], F32)
    go = P.out("g", [D, TOK], F32)

    ln = LNCtx(P, TOK)
    lgbt = P.sb("lgbt", [128, 16], F32)
    bvt = P.sb("bvt", [128, 52], F32)
    harena = P.sb("harena", [128, 22 * 1024], BF16)
    ta = P.sb("ta", [128, 512], F32)
    tg = P.sb("tg", [128, 512], F32)
    t2 = [P.sb(f"t2{i}", [128, 1024], F32) for i in range(2)]
    gtl = [P.sb(f"gtl{i}", [128, 1024], F32) for i in range(2)]
    xr = [P.sb(f"xr{i}", [128, 1024], F32) for i in range(2)]

    ld = P.sem("ld")
    ldg = P.sem("ldg")
    P.y.dma_start(out=lgbt[:, :], in_=lgb[:, :]).then_inc(ld, 16)
    P.y.dma_start(out=bvt[:, :], in_=bvec[:, :]).then_inc(ld, 16)
    P.g.dma_start(out=ln.ones128[:, :], in_=one_col[:, :]).then_inc(ldg, 16)
    P.y.dma_start(out=ln.onesK1[:, :], in_=ones_row[:, :]).then_inc(ld, 16)
    P.y.dma_start(out=ln.eps[:, :], in_=eps_col[:, :]).then_inc(ld, 16)
    for kc in range(NCH):
        c0 = 1 if kc < 4 else 0
        P.g.dma_start(out=chunk_view(ln.xb, NCH, TOK, kc, 0, TOK),
                      in_=xh[kc * 128:(kc + 1) * 128, c0:c0 + TOK]).then_inc(ldg, 16)
    for e in (P.t, P.v, P.s):
        e.wait(ld, ld.n)
        e.wait(ldg, ldg.n)
    done = emit_ln(P, "b", ln, None, ln.xb[:, :])
    P.v.wait(done, done.n)
    afd = P.sem("afd")
    for kc in range(NCH):
        lbb = bass.AP(lgbt, 8 + kc, [[16, 128], [0, TOK]])
        P.v.scalar_tensor_tensor(chunk_view(ln.xb, NCH, TOK, kc, 0, TOK),
                                 chunk_view(ln.xb, NCH, TOK, kc, 0, TOK),
                                 lgbt[:, kc:kc + 1], lbb, ALU.mult, ALU.add)
    P.vfence(afd, 1)
    P.t.wait(afd, afd.n)

    poolA = WPool(P, "a", 8 * 128, nbuf=2)
    poolB = WPool(P, "b", 8 * 128, nbuf=2)
    pool2 = WPool(P, "m", 22 * 128, nbuf=2)
    cmb = P.sem("cmb")
    e2 = P.sem("e2")
    mm2d = None
    ldr = [P.sem(f"ldr{i}") for i in range(2)]
    stx = [P.sem(f"stx{i}") for i in range(2)]
    stg = [P.sem(f"stg{i}") for i in range(2)]
    u2 = [0, 0]
    for hf in range(2):
        if hf == 1:
            P.v.wait(mm2d, mm2d.n)

        def xvh(kc, t0, t1, hf=hf):
            return chunk_view(ln.xb, NCH, TOK, kc, hf * 1024 + t0, hf * 1024 + t1)

        def combine(mt, tb, psA, psB, mmsem, mmval):
            P.s.wait(cmb, cmb.n)
            P.s.wait(mmsem, mmval)
            P.s.activation(ta[:, :], psA[:, :], AF.Identity, bias=bvt[:, mt:mt + 1])
            P.s.activation(tg[:, :], psB[:, :], AF.Gelu,
                           bias=bvt[:, 22 + mt:22 + mt + 1]).then_inc(cmb, 1)
            P.v.wait(cmb, cmb.n)
            P.v.tensor_tensor(
                bass.AP(harena, mt * 1024 + tb * 512, [[22 * 1024, 128], [1, 512]]),
                ta[:, :], tg[:, :], ALU.mult).then_inc(cmb, 1)
            return (cmb, cmb.n)
        emit_gemm2(P, f"w1{hf}", poolA, poolB, w1a, w1g, D, 2816, xvh, 1024, combine,
                   banks=bk)
        P.t.wait(cmb, cmb.n)

        def evac2(mt, tb, ps_ap, mmsem, mmval, hf=hf):
            slot = mt % 2
            if tb == 0:
                if u2[slot] > 0:
                    P.y.wait(stx[slot], 16 * u2[slot])
                    P.y.wait(stg[slot], 16 * u2[slot])
                P.y.dma_start(out=xr[slot][:, :],
                              in_=xh[mt * 128:(mt + 1) * 128,
                                     1 + hf * 1024:1 + hf * 1024 + 1024]
                              ).then_inc(ldr[slot], 16)
            P.s.wait(mmsem, mmval)
            if u2[slot] > 0:
                P.s.wait(stx[slot], 16 * u2[slot])
                P.s.wait(stg[slot], 16 * u2[slot])
            P.s.activation(t2[slot][:, tb * 512:(tb + 1) * 512], ps_ap, AF.Identity,
                           bias=bvt[:, 44 + mt:44 + mt + 1]).then_inc(e2, 1)
            if tb == 1:
                P.v.wait(ldr[slot], ldr[slot].n)
                P.v.wait(e2, e2.n)
                P.v.tensor_tensor(t2[slot][:, :], t2[slot][:, :], xr[slot][:, :],
                                  ALU.add)
                P.vfence(e2, 1)
                P.s.wait(e2, e2.n)
                P.s.activation(gtl[slot][:, :], t2[slot][:, :], AF.Silu)
                P.sfence(e2, 1)
                P.y.wait(e2, e2.n)
                P.y.dma_start(out=xoo[mt * 128:(mt + 1) * 128,
                                      hf * 1024:hf * 1024 + 1024],
                              in_=t2[slot][:, :]).then_inc(stx[slot], 16)
                P.y.dma_start(out=go[mt * 128:(mt + 1) * 128,
                                     hf * 1024:hf * 1024 + 1024],
                              in_=gtl[slot][:, :]).then_inc(stg[slot], 16)
                u2[slot] += 1
            return (e2, e2.n)
        mm2d = emit_gemm(P, f"w2{hf}", pool2, w2, 2816, D,
                         lambda kc, t0, t1: bass.AP(harena, kc * 1024 + t0,
                                                    [[22 * 1024, 128], [1, t1 - t0]]),
                         1024, evac2, nsets=2, banks=bk[0:4])
        P.t.wait(e2, e2.n)
    P.y.wait(stx[0], stx[0].n)
    P.y.wait(stx[1], stx[1].n)
    P.y.wait(stg[0], stg[0].n)
    P.y.wait(stg[1], stg[1].n)
    return P


def build_l3gate():
    P = Prog("l3gate")
    bk = P.bank(8)
    P._ln_banks = bk[0:4]
    xt, xn, ln, done, st, cstt = emit_loadx_ln(P)
    wg = P.inp("wg_st", [8, D, 128], F32)
    bgv = P.inp("bgv", [128, 8], F32)
    xgo = P.out("xg", [D, TOK], F32)
    bgt = P.sb("bgt", [128, 8], F32)
    gt = P.sb("gtile", [128, 512], F32)
    txf = P.sb("txf", [128, 512], F32)
    rt = [P.sb(f"rt{i}", [128, TOK], F32) for i in range(2)]
    xf = [P.sb(f"xf{i}", [128, TOK], F32) for i in range(2)]
    ld2 = P.sem("ld2")
    ldx = [P.sem(f"ldx{i}") for i in range(2)]
    xap = P._dram["x"]
    P.y.dma_start(out=bgt[:, :], in_=bgv[:, :]).then_inc(ld2, 16)
    for e in (P.t, P.v, P.s):
        e.wait(ld2, ld2.n)
    P.t.wait(done, done.n)
    pool = WPool(P, "w", 8 * 128, nbuf=3)
    ev = P.sem("ev")
    stx = [P.sem(f"stx{i}") for i in range(2)]
    uses = [0, 0]
    mean_b = lambda t0, t1: ln.mean[:, t0:t1]
    rstd_b = lambda t0, t1: ln.rstd[:, t0:t1]

    def evac(mt, tb, ps_ap, mmsem, mmval):
        slot = mt % 2
        if tb == 0:
            if uses[slot] > 0:
                P.y.wait(stx[slot], 16 * uses[slot])
            P.y.dma_start(out=xf[slot][:, :],
                          in_=xap[mt * 128:(mt + 1) * 128, :]).then_inc(ldx[slot], 16)
        P.s.wait(mmsem, mmval)
        P.s.wait(ev, ev.n)
        P.s.activation(gt[:, :], ps_ap, AF.Sigmoid,
                       bias=bgt[:, mt:mt + 1]).then_inc(ev, 1)
        P.v.wait(ev, ev.n)
        if tb == 0:
            P.v.wait(ldx[slot], ldx[slot].n)
            if uses[slot] > 0:
                P.v.wait(stx[slot], 16 * uses[slot])
        t0, t1 = tb * 512, (tb + 1) * 512
        P.v.tensor_tensor(txf[:, :], xf[slot][:, t0:t1], mean_b(t0, t1), ALU.subtract)
        P.v.tensor_tensor(txf[:, :], txf[:, :], rstd_b(t0, t1), ALU.mult)
        P.v.tensor_tensor(rt[slot][:, t0:t1], txf[:, :], gt[:, :],
                          ALU.mult).then_inc(ev, 1)
        if tb == 3:
            P.vfence(ev, 1)
            P.y.wait(ev, ev.n)
            P.y.dma_start(out=xgo[mt * 128:(mt + 1) * 128, :],
                          in_=rt[slot][:, :]).then_inc(stx[slot], 16)
            uses[slot] += 1
        return (ev, ev.n)

    emit_gemm(P, "wg", pool, wg, D, D, chunk_viewer(xn, NCH, TOK), TOK, evac,
              nsets=1, banks=bk[4:8])
    P.y.wait(stx[0], stx[0].n)
    P.y.wait(stx[1], stx[1].n)
    P.y.wait(P._sems["st_head"], P._sems["st_head"].n)
    return P


def build_l3post():
    P = Prog("l3post")
    yi = P.inp("y", [D, TOK], F32)
    xg = P.inp("xg", [D, TOK], F32)
    x3 = P.inp("x", [D, TOK], F32)
    g3 = P.inp("g", [D, TOK], F32)
    gb = P.inp("gb", [128, 16], F32)
    one_col = P.inp("one_col", [128, 1], F32)
    eps_col = P.inp("eps_col", [128, 1], F32)
    ones_row = P.inp("ones_row", [1, 128], F32)
    xoo = P.out("xo", [D, TOK], F32)
    ln = LNCtx(P, TOK)
    gbt = P.sb("gbt", [128, 16], F32)
    ty = [P.sb(f"ty{i}", [128, TOK], F32) for i in range(2)]
    tx = [P.sb(f"tx{i}", [128, TOK], F32) for i in range(2)]
    tx3 = [P.sb(f"tx3{i}", [128, TOK], F32) for i in range(2)]
    tg3 = [P.sb(f"tg3{i}", [128, TOK], F32) for i in range(2)]
    ld = P.sem("ld")
    ldg = P.sem("ldg")
    cd = P.sem("cd")
    P.y.dma_start(out=gbt[:, :], in_=gb[:, :]).then_inc(ld, 16)
    P.g.dma_start(out=ln.ones128[:, :], in_=one_col[:, :]).then_inc(ldg, 16)
    P.y.dma_start(out=ln.onesK1[:, :], in_=ones_row[:, :]).then_inc(ld, 16)
    P.y.dma_start(out=ln.eps[:, :], in_=eps_col[:, :]).then_inc(ld, 16)
    P.v.wait(ld, ld.n)
    P.v.wait(ldg, ldg.n)
    lds = [P.sem(f"lds{i}") for i in range(2)]
    for kc in range(NCH):
        par = kc % 2
        if kc >= 2:
            P.y.wait(cd, cd.n - 1)
        P.y.dma_start(out=ty[par][:, :], in_=yi[kc * 128:(kc + 1) * 128, :]).then_inc(lds[par], 16)
        P.y.dma_start(out=tx[par][:, :], in_=xg[kc * 128:(kc + 1) * 128, :]).then_inc(lds[par], 16)
        P.v.wait(lds[par], lds[par].n)
        P.v.tensor_tensor(chunk_view(ln.xb, NCH, TOK, kc, 0, TOK),
                          ty[par][:, :], tx[par][:, :], ALU.add)
        P.vfence(cd, 1)
    for e in (P.t, P.s):
        e.wait(ld, ld.n)
        e.wait(ldg, ldg.n)
        e.wait(cd, cd.n)
    done = emit_ln(P, "p", ln, None, ln.xb[:, :])
    P.v.wait(done, done.n)
    sts = [P.sem(f"sts{i}") for i in range(2)]
    uses = [0, 0]
    for kc in range(NCH):
        par = kc % 2
        if uses[par] > 0:
            P.v.wait(sts[par], 16 * uses[par])
        P.y.wait(cd, cd.n)
        P.y.dma_start(out=ty[par][:, :], in_=yi[kc * 128:(kc + 1) * 128, :]).then_inc(lds[par], 16)
        P.y.dma_start(out=tx[par][:, :], in_=xg[kc * 128:(kc + 1) * 128, :]).then_inc(lds[par], 16)
        P.y.dma_start(out=tx3[par][:, :], in_=x3[kc * 128:(kc + 1) * 128, :]).then_inc(lds[par], 16)
        P.y.dma_start(out=tg3[par][:, :], in_=g3[kc * 128:(kc + 1) * 128, :]).then_inc(lds[par], 16)
        gmb = bass.AP(gbt, 8 + kc, [[16, 128], [0, TOK]])
        P.v.wait(lds[par], lds[par].n)
        P.v.tensor_tensor(ty[par][:, :], ty[par][:, :], tx[par][:, :], ALU.add)
        P.v.tensor_tensor(ty[par][:, :], ty[par][:, :], ln.mean[:, :], ALU.subtract)
        P.v.tensor_tensor(ty[par][:, :], ty[par][:, :], ln.rstd[:, :], ALU.mult)
        P.v.scalar_tensor_tensor(ty[par][:, :], ty[par][:, :],
                                 gbt[:, kc:kc + 1], gmb, ALU.mult, ALU.add)
        P.v.tensor_tensor(ty[par][:, :], ty[par][:, :], tg3[par][:, :], ALU.mult)
        P.v.tensor_tensor(ty[par][:, :], ty[par][:, :], tx3[par][:, :], ALU.add)
        P.vfence(cd, 1)
        P.y.wait(cd, cd.n)
        P.y.dma_start(out=xoo[kc * 128:(kc + 1) * 128, :],
                      in_=ty[par][:, :]).then_inc(sts[par], 16)
        uses[par] += 1
    P.y.wait(sts[0], sts[0].n)
    P.y.wait(sts[1], sts[1].n)
    return P


def build_l3scan():
    P = Prog("l3scan")
    T, N = 32, 16
    NJ = S // T
    xg = P.inp("xg", [128, B * S], F32)
    ai = P.inp("ai", [128, 256], F32)
    bp = P.inp("bp", [128, N], F32)
    cp = P.inp("cp", [128, N], F32)
    dp = P.inp("dp", [128, 1], F32)
    yo = P.out("y", [128, B * S], F32)
    import os as _os1
    dbg = _os1.environ.get("SCANDBG") == "1"
    if dbg:
        tblo = P.out("tbl", [128, 1840], F32)

    xgt = P.sb("xgt", [128, B * S], F32)
    at = P.sb("at", [128, 256], F32)
    bpt = P.sb("bpt", [128, N], F32)
    cpt = P.sb("cpt", [128, N], F32)
    dpt = P.sb("dpt", [128, 1], F32)
    Ptab = P.sb("Ptab", [128, T * N], F32)
    htab = P.sb("htab", [128, T], F32)
    Utab = P.sb("Utab", [128, (T + 1) * N], F32)
    mats = {nm: P.sb(f"mat_{nm}", [128, 256], F32)
            for nm in ("A2", "A4", "A8", "A16", "M", "M2", "M3")}
    tt = P.sb("tt", [128, T * N], F32)
    yb = [P.sb(f"yb{i}", [128, S], F32) for i in range(2)]
    ctile = P.sb("ctile", [128, NJ * N], F32)
    setile = P.sb("setile", [128, NJ * N], F32)

    ld = P.sem("ld")
    cd = P.sem("cd")
    sts = [P.sem(f"sts{i}") for i in range(2)]
    P.y.dma_start(out=xgt[:, :], in_=xg[:, :]).then_inc(ld, 16)
    P.y.dma_start(out=at[:, :], in_=ai[:, :]).then_inc(ld, 16)
    P.y.dma_start(out=bpt[:, :], in_=bp[:, :]).then_inc(ld, 16)
    P.y.dma_start(out=cpt[:, :], in_=cp[:, :]).then_inc(ld, 16)
    P.y.dma_start(out=dpt[:, :], in_=dp[:, :]).then_inc(ld, 16)
    P.v.wait(ld, ld.n)
    v = P.v

    def arow(src, n):
        return src[:, n * N:(n + 1) * N]

    def acol(src, m):
        return bass.AP(src, m, [[256, 128], [N, N]])

    tfs = P.sem("tfs")

    def vdrain():
        P.vfence(tfs, 1)
        v.wait(tfs, tfs.n)
    v.tensor_copy(Ptab[:, 0:N], bpt[:, :])
    for j in range(1, T):
        vdrain()
        dst = Ptab[:, j * N:(j + 1) * N]
        for n in range(N):
            sc = Ptab[:, (j - 1) * N + n:(j - 1) * N + n + 1]
            if n == 0:
                v.tensor_single_scalar(dst, arow(at, 0), sc, ALU.mult)
            else:
                v.scalar_tensor_tensor(dst, arow(at, n), sc, dst, ALU.mult, ALU.add)
    v.tensor_tensor(bass.AP(tt, 0, [[T * N, 128], [N, T], [1, N]]),
                    bass.AP(Ptab, 0, [[T * N, 128], [N, T], [1, N]]),
                    bass.AP(cpt, 0, [[N, 128], [0, T], [1, N]]), ALU.mult)
    v.reduce_sum(bass.AP(htab, 0, [[T, 128], [1, T], [1, 1]]),
                 bass.AP(tt, 0, [[T * N, 128], [N, T], [1, N]]), AX.X)
    v.tensor_copy(Utab[:, 0:N], cpt[:, :])
    for t in range(1, T + 1):
        vdrain()
        dst = Utab[:, t * N:(t + 1) * N]
        for m in range(N):
            sc = Utab[:, (t - 1) * N + m:(t - 1) * N + m + 1]
            if m == 0:
                v.tensor_single_scalar(dst, acol(at, 0), sc, ALU.mult)
            else:
                v.scalar_tensor_tensor(dst, acol(at, m), sc, dst, ALU.mult, ALU.add)

    def matmat(dst, X, Y):
        for n in range(N):
            dr = arow(dst, n)
            for p_ in range(N):
                sc = X[:, n * N + p_:n * N + p_ + 1]
                if p_ == 0:
                    v.tensor_single_scalar(dr, arow(Y, 0), sc, ALU.mult)
                else:
                    v.scalar_tensor_tensor(dr, arow(Y, p_), sc, dr, ALU.mult, ALU.add)
    matmat(mats["A2"], at, at)
    matmat(mats["A4"], mats["A2"], mats["A2"])
    matmat(mats["A8"], mats["A4"], mats["A4"])
    matmat(mats["A16"], mats["A8"], mats["A8"])
    matmat(mats["M"], mats["A16"], mats["A16"])
    matmat(mats["M2"], mats["M"], mats["M"])
    matmat(mats["M3"], mats["M2"], mats["M"])

    uses = [0, 0]
    for b in range(B):
        par = b % 2
        x0 = b * S
        if uses[par] > 0:
            v.wait(sts[par], 16 * uses[par])
        v.tensor_single_scalar(yb[par][:, :], xgt[:, x0:x0 + S], htab[:, 0:1], ALU.mult)
        for l in range(1, T):
            dst = bass.AP(yb[par], l, [[S, 128], [T, NJ], [1, T - l]])
            src = bass.AP(xgt, x0, [[B * S, 128], [T, NJ], [1, T - l]])
            v.scalar_tensor_tensor(dst, src, htab[:, l:l + 1], dst, ALU.mult, ALU.add)
        for tau in range(T):
            src = bass.AP(xgt, x0 + tau, [[B * S, 128], [T, NJ]])
            for n in range(N):
                dst = bass.AP(ctile, n, [[NJ * N, 128], [N, NJ]])
                sc = Ptab[:, (T - 1 - tau) * N + n:(T - 1 - tau) * N + n + 1]
                if tau == 0:
                    v.tensor_single_scalar(dst, src, sc, ALU.mult)
                else:
                    v.scalar_tensor_tensor(dst, src, sc, dst, ALU.mult, ALU.add)
        v.tensor_copy(setile[:, :], ctile[:, :])
        for p_, mn in ((1, "M"), (2, "M2"), (3, "M3")):
            Mp = mats[mn]
            for n in range(N):
                src = bass.AP(ctile, n, [[NJ * N, 128], [N, NJ - p_]])
                for m in range(N):
                    dst = bass.AP(setile, p_ * N + m, [[NJ * N, 128], [N, NJ - p_]])
                    v.scalar_tensor_tensor(dst, src, Mp[:, n * N + m:n * N + m + 1],
                                           dst, ALU.mult, ALU.add)
        for t in range(1, T + 1):
            dst = bass.AP(yb[par], T + (t - 1), [[S, 128], [T, NJ - 1]])
            for n in range(N):
                src = bass.AP(setile, n, [[NJ * N, 128], [N, NJ - 1]])
                v.scalar_tensor_tensor(dst, src, Utab[:, t * N + n:t * N + n + 1],
                                       dst, ALU.mult, ALU.add)
        v.scalar_tensor_tensor(yb[par][:, :], xgt[:, x0:x0 + S], dpt[:, 0:1],
                               yb[par][:, :], ALU.mult, ALU.add)
        P.vfence(cd, 1)
        P.y.wait(cd, cd.n)
        P.y.dma_start(out=yo[:, x0:x0 + S], in_=yb[par][:, :]).then_inc(sts[par], 16)
        uses[par] += 1
    if dbg:
        P.y.wait(cd, cd.n)
        col = 0
        for t_, w in ((Ptab, T * N), (htab, T), (Utab, (T + 1) * N),
                      (mats["M"], 256), (mats["M2"], 256), (mats["M3"], 256)):
            P.y.dma_start(out=tblo[:, col:col + w], in_=t_[:, 0:w]).then_inc(sts[0], 16)
            col += w
    P.y.wait(sts[0], sts[0].n)
    P.y.wait(sts[1], sts[1].n)
    return P


_IA_CACHE = {}


def ia_tables(W, core):
    key = ('tab', W, core % 2)
    if key in _IA_CACHE:
        return _IA_CACHE[key]
    DH = 64
    halfd = np.arange(0, DH, 2, dtype=np.float32)
    inv_freq = 1.0 / 10000.0 ** (halfd / DH)
    n0 = (core % 2) * TOK
    t = np.arange(n0, n0 + TOK, dtype=np.float32)
    fr = t[:, None] * inv_freq[None, :]
    base = (halfd + 0.4 * DH) / (1.4 * DH)
    power = (t - S // 2) / (W // 2)
    sc = base[None, :] ** power[:, None]
    cos, sin = np.cos(fr), np.sin(fr)
    sgn = np.concatenate([-np.ones(32, np.float32), np.ones(32, np.float32)])
    sgn = np.tile(sgn, 2)[:, None]

    def rep(m):
        v = np.concatenate([m.T, m.T], 0)
        return np.concatenate([v, v], 0).astype(np.float32)
    qcs = rep(cos * sc * 8.0)
    qss = rep(sin * sc * 8.0) * sgn
    kcs = rep(cos / sc)
    kss = rep(sin / sc) * sgn
    out = _c(np.concatenate([qcs, qss, kcs, kss], 1))
    _IA_CACHE[key] = out
    return out


def ia_masks(W, core):
    key = ('msk', W, core % 2)
    if key in _IA_CACHE:
        return _IA_CACHE[key]
    nkt = (512 + W) // 128
    half = core % 2
    m = np.zeros((4, nkt, 128, 512), np.float32)
    for qb in range(4):
        q0 = qb * 512
        for kt in range(nkt):
            kl = q0 + kt * 128 + np.arange(128)[:, None]
            i = q0 + np.arange(512)[None, :]
            jloc = kl - W
            valid = (i - jloc >= 0) & (i - jloc <= W)
            if half == 0:
                valid = valid & (jloc >= 0)
            m[qb, kt] = valid
    out = _c(((m - 1.0) * 1e9).reshape(4 * nkt * 128, 512).astype(BF))
    _IA_CACHE[key] = out
    return out


def ia_vt(vh, heads):
    KTOT = vh.shape[1]
    vt = np.zeros((KTOT, heads, 65), BF)
    for h in range(heads):
        vt[:, h, 0:64] = vh[h * 64:(h + 1) * 64, :].T
    vt[:, :, 64] = np.float32(1.0)
    return _c(vt.reshape(KTOT, heads * 65))


def add_halo1(x_parts):
    out = []
    for c in range(NCORES):
        if c % 2 == 0:
            halo = np.zeros((D, 1), np.float32)
        else:
            halo = np.asarray(x_parts[c - 1], np.float32)[:, -1:]
        out.append(_c(np.concatenate([halo, np.asarray(x_parts[c], np.float32)], 1)))
    return out


def add_halo_w(parts, W):
    out = []
    for c in range(NCORES):
        I = parts[c].shape[0]
        if c % 2 == 0:
            halo = np.zeros((I, W), BF)
        else:
            halo = np.asarray(parts[c - 1])[:, TOK - W:]
        out.append(_c(np.concatenate([halo, np.asarray(parts[c])], 1)))
    return out


def run_inner_layer(progs, x_parts, heads, W, wqkv, qs, ks, wo, lg, lb, w1, b1, w2, b2):
    A1, A2, A3, Bp_ = progs
    wqkv_st = strips(wqkv)
    wo_st = strips(wo)
    ind2 = np.zeros((128, 2), np.float32)
    ind2[0:64, 0] = 1
    ind2[64:128, 1] = 1
    bc2 = _c(ind2.T.astype(np.float32))
    ind2b = ind2.astype(BF)
    qksc = _c(np.stack([np.tile(np.asarray(qs, np.float32), 2),
                        np.tile(np.asarray(ks, np.float32), 2)], 1))
    xh = add_halo1(x_parts)
    maps1 = [{"xh": xh[c], "wqkv_st": wqkv_st, "tabs": ia_tables(W, c),
              "ind2": ind2b, "bc2": bc2, "qksc": qksc} for c in range(NCORES)]
    r1 = run_spmd(A1, maps1).results
    qr = [np.asarray(r1[c]['qr']) for c in range(NCORES)]
    kr = [np.asarray(r1[c]['kr']) for c in range(NCORES)]
    vv = [np.asarray(r1[c]['vv']) for c in range(NCORES)]
    khs = add_halo_w(kr, W)
    vhs = add_halo_w(vv, W)
    ones_row = np.ones((1, 128), np.float32)
    maps2 = [{"qf": qr[c], "kh": khs[c], "vt": ia_vt(vhs[c], heads),
              "lmask": ia_masks(W, c), "ones_row": ones_row} for c in range(NCORES)]
    r2 = run_spmd(A2, maps2).results
    maps3 = [{"o": np.asarray(r2[c]['att']), "x": np.asarray(x_parts[c], np.float32),
              "wo_st": wo_st} for c in range(NCORES)]
    r3 = run_spmd(A3, maps3).results
    xmid = [np.asarray(r3[c]['xo']) for c in range(NCORES)]
    w1a = np.zeros((D, 2816), np.float32)
    w1a[:, 0:2730] = w1[:, 0:2730]
    w1g = np.zeros((D, 2816), np.float32)
    w1g[:, 0:2730] = w1[:, 2730:5460]
    b1a = np.zeros(2816, np.float32)
    b1a[0:2730] = b1[0:2730]
    b1g = np.zeros(2816, np.float32)
    b1g[0:2730] = b1[2730:]
    w2p = np.zeros((2816, D), np.float32)
    w2p[0:2730] = w2
    bvec = np.concatenate([vec_fm(b1a), vec_fm(b1g), vec_fm(np.asarray(b2))], 1)
    lgbv = np.concatenate([vec_fm(np.asarray(lg)), vec_fm(np.asarray(lb))], 1)
    cst = {"one_col": np.ones((128, 1), np.float32),
           "eps_col": np.full((128, 1), 1e-5, np.float32),
           "ones_row": ones_row}
    xh2 = add_halo1(xmid)
    w1a_st, w1g_st, w2_st = strips(w1a), strips(w1g), strips(w2p)
    mapsB = [dict(xh=xh2[c], w1a_st=w1a_st, w1g_st=w1g_st, w2_st=w2_st,
                  lgb=lgbv, bvec=bvec, **cst) for c in range(NCORES)]
    rB = run_spmd(Bp_, mapsB).results
    xo = [np.asarray(rB[c]['xo']) for c in range(NCORES)]
    g = [np.asarray(rB[c]['g']) for c in range(NCORES)]
    return xo, g


_PROGS = {}


def _prog(name, builder, *a):
    if name not in _PROGS:
        _PROGS[name] = builder(*a)
    return _PROGS[name]


def _f32(a):
    return np.asarray(a, np.float32)


def _run_blockfin(x_parts, s_parts, g_parts):
    P = _prog("blockfin", build_blockfin)
    maps = [{"x": _c(_f32(x_parts[c])), "s": s_parts[c], "g": _c(_f32(g_parts[c]))}
            for c in range(NCORES)]
    r = run_spmd(P, maps).results
    return [_f32(r[c]['xo']) for c in range(NCORES)]


def kernel(**inputs):
    inputs = {k: np.asarray(v) for k, v in inputs.items()}
    x_fm = fm_slices(inputs['L0_attn_x'].astype(np.float32))
    ones_row = np.ones((1, 128), np.float32)

    r1 = run_spmd(_prog("l0proj", build_l0proj), l0_proj_maps(inputs, x_fm)).results
    rwq = run_spmd(_prog("l0wq", build_l0wq), l0_wq_maps(inputs)).results
    wgq_st, wuq_st, wdq_st = l0_wq_assemble([rwq[c]['wq'] for c in range(NCORES)])
    xq_maps = [{"x": x_fm[c], "mr": np.asarray(r1[c]['mr']), "idf": IDENT128,
                "ones_row": ones_row} for c in range(NCORES)]
    rxq = run_spmd(_prog("l0xq", build_l0xq), xq_maps).results
    gu_maps = [{"xq": np.asarray(rxq[c]['xq']), "wgq_st": wgq_st, "wuq_st": wuq_st}
               for c in range(NCORES)]
    rgu = run_spmd(_prog("l0gu", build_l0gu), gu_maps).results
    hq_maps = [{"h": np.asarray(rgu[c]['h']), "idf": IDENT128} for c in range(NCORES)]
    rhq = run_spmd(_prog("l0hq", build_l0hq), hq_maps).results
    dn_maps = [{"hq": np.asarray(rhq[c]['hq']), "wdq_st": wdq_st} for c in range(NCORES)]
    rdn = run_spmd(_prog("l0dn", build_l0dn), dn_maps).results
    ra = run_spmd(_prog("l0attnA", build_l0attnA), l0_attnA_maps(r1)).results
    rb = run_spmd(_prog("l0attnB", build_l0attnB),
                  l0_attnB_maps(inputs, r1, ra)).results
    x1 = _run_blockfin(x_fm, [np.asarray(rb[c]['s0']) for c in range(NCORES)],
                       [np.asarray(rdn[c]['g0']) for c in range(NCORES)])

    rp = run_spmd(_prog("l1proj", build_lnproj3), l1_host(inputs, x1)).results
    qs = [rp[c]['q'] for c in range(NCORES)]
    ks = [rp[c]['k'] for c in range(NCORES)]
    vs = [rp[c]['v'] for c in range(NCORES)]
    rat = run_spmd(_prog("l1attn", build_l1attn), l1_attn_maps(qs, ks, vs)).results
    attns = [rat[c]['attn'] for c in range(NCORES)]
    rt = run_spmd(_prog("l1tail", build_l1tail),
                  l1_tail_maps(inputs, [rp[c]['xn'] for c in range(NCORES)],
                               attns)).results
    x2 = _run_blockfin(x1, [np.asarray(rt[c]['s1']) for c in range(NCORES)],
                       [np.asarray(rt[c]['g1']) for c in range(NCORES)])

    r2q = run_spmd(_prog("l2qkv", build_l2qkv), l2_qkv_maps(inputs, x2)).results
    r2a = run_spmd(_prog("l2attn", build_l2attn), l2_attn_maps(inputs, r2q)).results
    r2g = run_spmd(_prog("l2gr", build_l2gr), l2_gr_maps(inputs, r2a)).results
    s2 = l2_s2_assemble(inputs, r2g)
    s2_parts = [_c(p.astype(BF)) for p in fm_slices(s2)]
    l2fp = (_prog("l2fA1", build_ifA1, 16, "l2fA1"),
            _prog("l2fA2", build_ifA2, 16, 256, "l2fA2"),
            _prog("l2fA3", build_ifA3, 16, "l2fA3"),
            _prog("ifB", build_ifB))
    xi = [_f32(r2q[c]['xn']) for c in range(NCORES)]
    for i in range(6):
        xi, gl = run_inner_layer(
            l2fp, xi, 16, 256,
            inputs['L2_ffn_Wqkv'][i], inputs['L2_ffn_q_scale'][i],
            inputs['L2_ffn_k_scale'][i], inputs['L2_ffn_Wo'][i],
            inputs['L2_ffn_ln_g'][i], inputs['L2_ffn_ln_b'][i],
            inputs['L2_ffn_W1'][i], inputs['L2_ffn_b1'][i],
            inputs['L2_ffn_W2'][i], inputs['L2_ffn_b2'][i])
    x3 = _run_blockfin(x2, s2_parts, gl)

    wg_st = strips(_c(inputs['L3_attn_Wg'].T))
    bgv = vec_fm(inputs['L3_attn_bg'])
    gmaps = []
    for c in range(NCORES):
        m = host_head_inputs(_c(x3[c]))
        m.update({"wg_st": wg_st, "bgv": bgv})
        gmaps.append(m)
    rg = run_spmd(_prog("l3gate", build_l3gate), gmaps).results
    xg = [_f32(rg[c]['xg']) for c in range(NCORES)]
    l3fp = (_prog("l3fA1", build_ifA1, 8, "l3fA1"),
            _prog("l3fA2", build_ifA2, 8, 512, "l3fA2"),
            _prog("l3fA3", build_ifA3, 8, "l3fA3"),
            _prog("ifB", build_ifB))
    xi = [_f32(rg[c]['xn']) for c in range(NCORES)]
    for i in range(2):
        xi, gl = run_inner_layer(
            l3fp, xi, 8, 512,
            inputs['L3_ffn_wqkv'][i], inputs['L3_ffn_q_scale'][i],
            inputs['L3_ffn_k_scale'][i], inputs['L3_ffn_wo'][i],
            inputs['L3_ffn_ln_g'][i], inputs['L3_ffn_ln_b'][i],
            inputs['L3_ffn_w1'][i], inputs['L3_ffn_b1'][i],
            inputs['L3_ffn_w2'][i], inputs['L3_ffn_b2'][i])
    xg_glob = np.zeros((1024, B * S), np.float32)
    for c in range(NCORES):
        b, h = c // 2, c % 2
        xg_glob[:, b * S + h * TOK: b * S + (h + 1) * TOK] = xg[c]
    A2d = inputs['L3_attn_A'].reshape(1024, 256)
    smaps = [{"xg": _c(xg_glob[c * 128:(c + 1) * 128]),
              "ai": _c(A2d[c * 128:(c + 1) * 128]),
              "bp": _c(inputs['L3_attn_Bp'][c * 128:(c + 1) * 128]),
              "cp": _c(inputs['L3_attn_Cp'][c * 128:(c + 1) * 128]),
              "dp": _c(inputs['L3_attn_Dp'][c * 128:(c + 1) * 128, None])}
             for c in range(NCORES)]
    rs = run_spmd(_prog("l3scan", build_l3scan), smaps).results
    y_glob = np.concatenate([_f32(rs[c]['y']) for c in range(NCORES)], 0)
    gb = np.concatenate([vec_fm(inputs['L3_attn_gamma']),
                         vec_fm(inputs['L3_attn_beta'])], 1)
    cst = {"one_col": np.ones((128, 1), np.float32),
           "eps_col": np.full((128, 1), 1e-5, np.float32),
           "ones_row": ones_row}
    pmaps = []
    for c in range(NCORES):
        b, h = c // 2, c % 2
        pmaps.append(dict(y=_c(y_glob[:, b * S + h * TOK: b * S + (h + 1) * TOK]),
                          xg=xg[c], x=_c(x3[c]), g=_c(gl[c]), gb=gb, **cst))
    rpz = run_spmd(_prog("l3post", build_l3post), pmaps).results
    return fm_unslice([_f32(rpz[c]['xo']) for c in range(NCORES)]).astype(np.float32)
